# version 11
# baseline (speedup 1.0000x reference)
import contextlib
import numpy as np
import ml_dtypes
import concourse.bass as bass
import concourse.mybir as mybir
from concourse.bass_utils import run_bass_kernel_spmd

F32 = mybir.dt.float32
BF16 = mybir.dt.bfloat16
AF = mybir.ActivationFunctionType
ALU = mybir.AluOpType
AX = mybir.AxisListType

NT = 18
NTOK = NT * 128
OWN = list(range(1, 17))
D = 1024
KC = 8
NE = 16
NIT = 24
NH = 16
SLAB = 384

ENGS = ("sync", "tensor", "scalar", "vector", "gpsimd")


class Step:
    __slots__ = ("eng", "fn", "waits", "val", "pre", "block", "name", "sem")


class Sched:

    def __init__(self):
        self.steps = []
        self.ctr = {e: 0 for e in ENGS}
        self.block = 0
        self.semnames = list(ENGS)

    def dma_sem(self, name):
        if name not in self.ctr:
            self.ctr[name] = 0
            self.semnames.append(name)
        return name

    def add(self, eng, fn, waits=(), dma=0, name="", sem=None):
        s = Step()
        s.eng = eng
        s.fn = fn
        s.name = name
        s.block = self.block
        if dma:
            assert sem is not None, name
            s.sem = self.dma_sem(sem)
        else:
            assert sem is None
            s.sem = eng
        ws = {}
        for w in waits:
            if w is None:
                continue
            we, wv = w
            assert wv <= self.ctr[we], (name, we, wv, self.ctr[we])
            if wv > ws.get(we, -1):
                ws[we] = wv
        s.waits = ws
        s.pre = self.ctr[s.sem]
        self.ctr[s.sem] += 16 * dma if dma else 1
        s.val = self.ctr[s.sem]
        self.steps.append(s)
        return (s.sem, s.val)

    def total(self, sem):
        return (sem, self.ctr[sem])

    def fence(self, eng, waits, name="fence"):
        s = Step()
        s.eng = eng
        s.sem = eng
        s.fn = None
        s.name = name
        s.block = self.block
        ws = {}
        for w in waits:
            if w is None:
                continue
            we, wv = w
            if wv > ws.get(we, -1):
                ws[we] = wv
        s.waits = ws
        s.pre = s.val = self.ctr[eng]
        self.steps.append(s)
        return None

    def new_block(self):
        self.block += 1

    def emit(self, nc, sems):
        nblocks = self.block + 1
        for bi in range(nblocks):
            bsteps = [s for s in self.steps if s.block == bi]
            if not bsteps:
                continue
            with nc.Block() as blk:
                for eng in ENGS:
                    esteps = [s for s in bsteps if s.eng == eng]
                    if not esteps:
                        continue

                    def body(e, esteps=esteps, eng=eng):
                        last_w = {}
                        for s in esteps:
                            for we, wv in sorted(s.waits.items()):
                                if wv > last_w.get(we, -1):
                                    e.wait_ge(sems[we], wv)
                                    last_w[we] = wv
                            if s.fn is None:
                                continue
                            out = s.fn(e)
                            if isinstance(out, list):
                                for ins in out:
                                    ins.then_inc(sems[s.sem], 16)
                            else:
                                out.then_inc(sems[s.sem], 1)
                    getattr(blk, eng)(body)


def build_nc(upto="full"):
    nc = bass.Bass("TRN2", target_bir_lowering=False,
                   dynamic_dma_scratch_size=16384)

    T = {}
    def dram(name, shape, dt=F32, kind=None):
        if kind:
            T[name] = nc.dram_tensor(name, shape, dt, kind=kind)
        else:
            T[name] = nc.dram_tensor(name, shape, dt)
        return T[name]

    dram("x_in", [NTOK, D], kind="ExternalInput")
    dram("Wqkv", [D, 3 * D], kind="ExternalInput")
    dram("bqkv", [3 * D], kind="ExternalInput")
    dram("Wout", [D, D], kind="ExternalInput")
    dram("bout", [D], kind="ExternalInput")
    dram("router_w", [16, D], kind="ExternalInput")
    dram("cbias", [16], kind="ExternalInput")
    dram("cb128", [128, 16], kind="ExternalInput")
    dram("gate_w", [16, D, 256], kind="ExternalInput")
    dram("up_w", [16, D, 256], kind="ExternalInput")
    dram("down_w", [16, 256, D], kind="ExternalInput")
    dram("sgw", [2048, D], kind="ExternalInput")
    dram("suw", [2048, D], kind="ExternalInput")
    dram("sdw", [D, 2048], kind="ExternalInput")
    dram("identb", [128, 128], BF16, kind="ExternalInput")
    dram("identf", [128, 128], kind="ExternalInput")
    dram("onehot", [16, 2048], BF16, kind="ExternalInput")
    dram("onesb", [128, 128], BF16, kind="ExternalInput")
    dram("triL", [128, 128], BF16, kind="ExternalInput")
    dram("triR", [128, 128], BF16, kind="ExternalInput")
    dram("mLe", [128, 128], BF16, kind="ExternalInput")
    dram("mRe", [128, 128], BF16, kind="ExternalInput")

    out_rows = 2048 if upto == "full" else NTOK
    dram("y", [out_rows, D], kind="ExternalOutput")
    if upto == "dbgscr":
        dram("yscr", [NIT * D + 48 * 128, 512], BF16, kind="ExternalOutput")
    if upto == "dbgff":
        dram("yxn", [128, KC * NTOK], BF16, kind="ExternalOutput")
        dram("yf", [128, 288 * 6], kind="ExternalOutput")

    dram("s_geu", [NIT, 128, 4096], BF16)
    dram("s_down", [48 * 128, D], BF16)
    dram("s_wqkvQK", [16, 128, 8 * 128], BF16)
    dram("s_wqkvV", [D, D], BF16)
    dram("s_wout", [D, D], BF16)
    dram("q_scr", [8, 128, 2048], BF16)

    S = Sched()
    es = contextlib.ExitStack()
    with es:
        sbuf = lambda name, shape, dt=F32: es.enter_context(nc.sbuf_tensor(name, shape, dt))

        G = {}
        G["X"] = sbuf("X", [128, NT * D])
        for n, sh, dt in [("identb", [128, 128], BF16), ("identf", [128, 128], F32),
                          ("onehot", [16, 2048], BF16), ("onesb", [128, 128], BF16),
                          ("triL", [128, 128], BF16), ("triR", [128, 128], BF16),
                          ("mLe", [128, 128], BF16), ("mRe", [128, 128], BF16)]:
            G[n] = sbuf("c_" + n, sh, dt)
        G["rwT"] = sbuf("c_rwT", [128, 8 * 16], BF16)
        G["cb"] = sbuf("c_cb", [128, 16])
        G["bqk"] = sbuf("c_bqk", [128, 16])
        G["bvb"] = sbuf("c_bvb", [1, D], BF16)
        G["bob"] = sbuf("c_bob", [1, D], BF16)
        _c2 = sbuf("c_eps2", [128, 2])
        G["eps"] = _c2[:, 0:1]
        G["s32"] = _c2[:, 1:2]
        for n in ("sum", "sq", "mean", "rstd", "nmr"):
            G["st_" + n] = sbuf("st_" + n, [128, NT])


        _prologue_early(nc, S, T, G, es)
        with contextlib.ExitStack() as es_cast:
            shared_ready = _cast_shared(nc, S, T, G, es_cast)
            pool_ready = _cast_pool(nc, S, T, G, es_cast)
            ready = dict(pool_ready)
            for kk in ("shared", "sd", "geu", "down"):
                ready[kk] = shared_ready[kk]
            G["_ready"] = ready

            import os as _os
            if _os.environ.get("KBAR"):
                S.new_block()
            if upto == "dbgscr":
                S.new_block()
                toks = []
                for b in range(NIT):
                    def f(e, b=b):
                        return [e.dma_start(out=T["yscr"][b * D:(b + 1) * D, :],
                                            in_=T["s_geu"][b, :, :])]
                    toks.append(S.add("sync", f, dma=1, sem="out", name="dscr"))
                def f2(e):
                    return [e.dma_start(out=T["yscr"][NIT * D:NIT * D + 48 * 128, 0:512],
                                        in_=T["s_down"][:, 0:512])]
                toks.append(S.add("sync", f2, dma=1, sem="out", name="dscr2"))
                S.fence("sync", toks, name="dscr_f")
            if upto.startswith("pro"):
                f1_done = S.total("ldx")
            else:
                f1_done = _ffn(nc, S, T, G, es, tiles=list(range(NT)), tag="f1",
                               prev=S.total("ldx"))
        S.new_block()

        if upto == "dbgff":
            def fd(e):
                return [e.dma_start(out=T["yxn"][:, :], in_=G["_dbg_xn"][:, :]),
                        e.dma_start(out=T["yf"][:, 0:288], in_=G["_dbg_sc"][:, :]),
                        e.dma_start(out=T["yf"][:, 288:576], in_=G["_dbg_ca"][:, :]),
                        e.dma_start(out=T["yf"][:, 576:594], in_=G["st_mean"][:, :]),
                        e.dma_start(out=T["yf"][:, 594:612], in_=G["st_rstd"][:, :]),
                        ]
            t = S.add("sync", fd, waits=[f1_done], dma=1, sem="out", name="dbgff")
            S.fence("sync", [t], name="dbgff_f")
        att_done = f1_done
        if upto in ("full", "attn"):
            att_done = _attention(nc, S, T, G, es, prev=f1_done)
            S.new_block()

        f2_done = att_done
        if upto == "full":
            f2_done = _ffn(nc, S, T, G, es, tiles=OWN, tag="f2", prev=att_done)
            S.new_block()

        def emit_out(t, i):
            def f(e):
                return [e.dma_start(out=T["y"][i * 128:(i + 1) * 128, :],
                                    in_=G["X"][:, t * D:(t + 1) * D])]
            return f
        tiles = OWN if upto == "full" else list(range(NT))
        outs = []
        for i, t in enumerate(tiles):
            outs.append(S.add("sync", emit_out(t, i), waits=[f2_done], dma=1, sem="out", name="out"))
        S.fence("sync", outs, name="out_fence")

        sems = {n: es.enter_context(nc.semaphore("sem_" + n)) for n in S.semnames}
        S.emit(nc, sems)
    return nc


def _prologue_early(nc, S, T, G, es):
    X = G["X"]
    for t in range(NT):
        def f(e, t=t):
            return [e.dma_start(out=X[:, t * D:(t + 1) * D],
                                in_=T["x_in"][t * 128:(t + 1) * 128, :])]
        S.add("sync", f, dma=1, sem="ldx", name="ldX")
    for n in ("identb", "identf", "onehot", "onesb", "triL", "triR", "mLe", "mRe"):
        def f(e, n=n):
            return [e.dma_start(out=G[n][:, :], in_=T[n][:, :])]
        S.add("sync", f, dma=1, sem="ldc", name="ldC")

    with contextlib.ExitStack() as es2:
        sb2 = lambda name, shape, dt=F32: es2.enter_context(nc.sbuf_tensor(name, shape, dt))
        t24 = sb2("p_t24", [24, 128])
        t16 = sb2("p_t16", [16, 1024])
        t16b = sb2("p_t16b", [16, 1024], BF16)
        trow = sb2("p_trow", [1, 2048])
        pps = es2.enter_context(nc.psum_tensor("p_pps", [128, 512], F32))
        ppb = es2.enter_context(nc.psum_tensor("p_ppb", [128, 512], BF16))

        l1 = S.add("sync", lambda e: [
            e.dma_start(out=t24[0:16, :],
                        in_=T["bqkv"].rearrange("(c p) -> c p", p=128)[0:16, :]),
            e.dma_start(out=trow[0:1, 0:1024], in_=T["bqkv"][None, 2048:3072]),
            e.dma_start(out=trow[0:1, 1024:2048], in_=T["bout"][None, :]),
            e.dma_start(out=G["cb"][:, :], in_=T["cb128"][:, :]),
            e.dma_start(out=t16[:, :], in_=T["router_w"][:, :]),
        ], dma=5, sem="lds", name="ld_small")

        v1 = S.add("vector", lambda e: e.memset(G["eps"], 1e-5))
        S.add("vector", lambda e: e.memset(G["s32"], 1.0 / 32.0), waits=[v1])
        v2 = S.add("vector", lambda e: e.tensor_copy(out=G["bvb"][:, :], in_=trow[0:1, 0:1024]),
                   waits=[l1])
        v3 = S.add("vector", lambda e: e.tensor_copy(out=G["bob"][:, :], in_=trow[0:1, 1024:2048]),
                   waits=[l1])
        v4 = S.add("vector", lambda e: e.tensor_copy(out=t16b[:, :], in_=t16[:, :]), waits=[l1])
        p1 = S.add("tensor", lambda e: e.matmul(pps[:, 0:16], t24[0:16, :],
                                                G["identf"][0:16, 0:16],
                                                is_transpose=True, start=True, stop=True),
                   waits=[l1, S.total("ldc")], name="tr_bqk")
        def rw_tr(e):
            for k in range(KC):
                ins = e.matmul(ppb[:, k * 16:(k + 1) * 16], t16b[0:16, k * 128:(k + 1) * 128],
                               G["identb"][0:16, 0:16], is_transpose=True, start=True, stop=True)
            return ins
        p2 = S.add("tensor", rw_tr, waits=[v4, S.total("ldc")], name="tr_rw")
        bqk_ev = S.add("vector", lambda e: e.tensor_copy(out=G["bqk"][:, :], in_=pps[:, 0:16]),
                       waits=[p1])
        rw_ev = S.add("vector", lambda e: e.tensor_copy(out=G["rwT"][:, :], in_=ppb[:, 0:128]),
                      waits=[p2])
    G["_rw_ev"] = rw_ev
    G["_bqk_ev"] = bqk_ev


def _cast_shared(nc, S, T, G, es2):
    ready = {"shared": {}, "sd": {}}
    with contextlib.ExitStack() as es_ps:
        sb2 = lambda name, shape, dt=F32: es2.enter_context(nc.sbuf_tensor(name, shape, dt))
        fb2 = sb2("cs_fb", [128, 2 * 512])
        bb2 = sb2("cs_bb", [128, 2 * 512], BF16)
        sTd = sb2("cs_sTd", [128, 2 * 1024], BF16)
        pta = es_ps.enter_context(nc.psum_tensor("cs_pta", [128, 128], BF16))
        ptb = es_ps.enter_context(nc.psum_tensor("cs_ptb", [128, 128], BF16))
        pbank = [pta, ptb]

        sg_st = {}
        sd_st = {}
        cast_tok = {}
        tr_last = {}
        ev_hist = []
        st_slot = {0: None, 1: None}
        tr_idx = [0]
        ji = [0]
        ds_ctr = [0]
        ds_last = {}

        def _ds_store(fn, waits, ndma=1, name="cs_st"):
            k = ds_ctr[0]
            ds_ctr[0] += 1
            p = k % 2
            w = list(waits) + ([ds_last[p]] if p in ds_last else [])
            tok = S.add("scalar", fn, waits=w, dma=ndma,
                        sem="dsA" if p == 0 else "dsB", name=name)
            ds_last[p] = tok
            return tok

        def job(ld_fn, ntr, tr_in_fn, ev_out_fn, extra_w=(), st_fn=None):
            i = ji[0]
            ji[0] += 1
            off = (i % 2) * 512
            wt = [cast_tok[i - 2]] if (i - 2) in cast_tok else [G["_rw_ev"], G["_bqk_ev"]]
            dw = S.add("sync", lambda e: [ld_fn(e, fb2, off)],
                       waits=wt + list(extra_w), dma=1,
                       sem="fbA" if i % 2 == 0 else "fbB", name="cs_ld")
            cw = [dw]
            if i >= 2:
                cw.append(tr_last[i - 2])
            cv = S.add("vector", lambda e: e.tensor_copy(out=bb2[:, off:off + 512],
                                                         in_=fb2[:, off:off + 512]),
                       waits=cw, name="cs_cast")
            cast_tok[i] = cv
            toks = []
            if st_fn is not None:
                stok = _ds_store(lambda e: st_fn(e, bb2, off), [cv], name="cs_dst")
                tr_last[i] = stok
                return i, [stok]
            for u in range(ntr):
                ti = tr_idx[0]
                tr_idx[0] += 1
                w2 = [cv]
                if len(ev_hist) >= 2:
                    w2.append(ev_hist[-2])
                pt = pbank[ti % 2]
                ptk = S.add("tensor", lambda e, u=u, off=off, pt=pt:
                            e.matmul(pt[:, :], tr_in_fn(bb2, off, u), G["identb"][:, :],
                                     is_transpose=True, start=True, stop=True),
                            waits=w2, name="cs_tr")
                ev = S.add("scalar", lambda e, u=u, pt=pt: ev_out_fn(e, pt, u),
                           waits=[ptk], name="cs_ev")
                ev_hist.append(ev)
                toks.append(ev)
            tr_last[i] = toks[-1] if toks else cv
            return i, toks

        def store(st_fn, waits, ndma=1):
            return _ds_store(lambda e: st_fn(e), waits, ndma=ndma)

        def emit_geu(e_lo, e_hi):
            for e_i in range(e_lo, e_hi):
                for (wsrc, c0) in ((T["gate_w"], 0), (T["up_w"], 256)):
                    for kh in range(4):
                        def ld(e, fbuf, off, wsrc=wsrc, e_i=e_i, kh=kh):
                            src = wsrc[e_i, :, :].rearrange("(k p) c -> p k c", p=128)[
                                :, kh * 2:(kh + 1) * 2, :]
                            return e.dma_start(out=fbuf[:, off:off + 512]
                                               .rearrange("p (k c) -> p k c", c=256),
                                               in_=src)
                        def st(e, bbuf, off, e_i=e_i, c0=c0, kh=kh):
                            dst = T["s_geu"][e_i, :, :].rearrange(
                                "p (k c) -> p k c", c=512)[:, kh * 2:kh * 2 + 2, c0:c0 + 256]
                            return [e.dma_start(out=dst,
                                                in_=bbuf[:, off:off + 512]
                                                .rearrange("p (k c) -> p k c", c=256))]
                        _, tk = job(ld, 0, None, None, st_fn=st)
                ready["geu"][e_i] = tk[0]

        def emit_down():
            for e_i in range(NE):
                for hh in range(2):
                    for ch in range(2):
                        def ld(e, fbuf, off, e_i=e_i, hh=hh, ch=ch):
                            return e.dma_start(out=fbuf[:, off:off + 512],
                                               in_=T["down_w"][e_i, hh * 128:(hh + 1) * 128,
                                                               ch * 512:(ch + 1) * 512])
                        def st(e, bbuf, off, e_i=e_i, hh=hh, ch=ch):
                            return [e.dma_start(
                                out=T["s_down"][(2 * e_i + hh) * 128:(2 * e_i + hh + 1) * 128,
                                                ch * 512:(ch + 1) * 512],
                                in_=bbuf[:, off:off + 512])]
                        _, tk = job(ld, 0, None, None, st_fn=st)
                    ready["down"][2 * e_i + hh] = tk[0]

        ready["geu"] = {}
        ready["down"] = {}
        emit_geu(0, 4)
        for (wsrc, c0) in ((T["sgw"], 0), (T["suw"], 256)):
            for j in range(16):
                pe_i = j // 2
                toks2 = []
                for hf in range(2):
                    def ld(e, fbuf, off, wsrc=wsrc, j=j, hf=hf):
                        return e.dma_start(out=fbuf[:, off:off + 512],
                                           in_=wsrc[j * 128:(j + 1) * 128,
                                                    hf * 512:(hf + 1) * 512])
                    def trin(bbuf, off, k):
                        return bbuf[:, off + k * 128:off + (k + 1) * 128]
                    def evout(e, pt, k, j=j, hf=hf):
                        kk = hf * 4 + k
                        return e.copy(out=sTd[:, (j % 2) * 1024 + kk * 128:
                                              (j % 2) * 1024 + (kk + 1) * 128],
                                      in_=pt[:, :])
                    ew = [sg_st[j - 2]] if (j - 2) in sg_st else []
                    _, toks = job(ld, 4, trin, evout, extra_w=ew)
                    toks2.append(toks[-1])
                def stf(e, c0=c0, j=j, pe_i=pe_i):
                    base = c0 + (j % 2) * 128
                    dst = T["s_geu"][16 + pe_i, :, :].rearrange(
                        "p (k c) -> p k c", c=512)[:, :, base:base + 128]
                    src = sTd[:, (j % 2) * 1024:(j % 2) * 1024 + 1024].rearrange(
                        "p (k c) -> p k c", c=128)
                    return [e.dma_start(out=dst, in_=src)]
                tok = store(stf, [toks2[-1]], ndma=1)
                sg_st[j] = tok
                if j % 2 == 1:
                    ready["shared"][pe_i] = tok
            sg_st[-2], sg_st[-1] = sg_st[14], sg_st[15]
            for j in list(sg_st):
                if j >= 0:
                    del sg_st[j]
        emit_geu(4, NE)
        emit_down()
        for h in range(16):
            toks2 = []
            for hf in range(2):
                def ld(e, fbuf, off, h=h, hf=hf):
                    src = T["sdw"].rearrange("(q p) (h c) -> p h q c", p=128, c=128)[
                        :, h, hf * 4:(hf + 1) * 4, :]
                    return e.dma_start(out=fbuf[:, off:off + 512]
                                       .rearrange("p (q c) -> p q c", c=128), in_=src)
                def trin(bbuf, off, k):
                    return bbuf[:, off + k * 128:off + (k + 1) * 128]
                def evout(e, pt, k, h=h, hf=hf):
                    kk = hf * 4 + k
                    return e.copy(out=sTd[:, (h % 2) * 1024 + kk * 128:
                                          (h % 2) * 1024 + (kk + 1) * 128],
                                  in_=pt[:, :])
                ew = [sd_st[h - 2]] if (h - 2) in sd_st else [sg_st[-2 + (h % 2)]]
                _, toks = job(ld, 4, trin, evout, extra_w=ew)
                toks2.append(toks[-1])
            def stf(e, h=h):
                return [e.dma_start(out=T["s_down"][4096 + h * 128:4096 + (h + 1) * 128, :],
                                    in_=sTd[:, (h % 2) * 1024:(h % 2) * 1024 + 1024])]
            tok = store(stf, [toks2[-1]], ndma=1)
            sd_st[h] = tok
            ready["sd"][h] = tok
    return ready


def _cast_pool(nc, S, T, G, es):
    ready = {"geu": {}, "down": {}}
    fb3 = es.enter_context(nc.sbuf_tensor("cp_fb", [128, 2 * 256], F32))
    bb3 = es.enter_context(nc.sbuf_tensor("cp_bb", [128, 2 * 256], BF16))
    st_slot = {0: None, 1: None}
    ji = [0]

    def job2(ld_fn, st_fn, name):
        i = ji[0]
        ji[0] += 1
        off = (i % 2) * 256
        wt = [st_slot[i % 2]] if st_slot[i % 2] else [G["_rw_ev"], G["_bqk_ev"]]
        dw = S.add("gpsimd", lambda e: [ld_fn(e, fb3, off)], waits=wt, dma=1,
                   sem="gpdA" if i % 2 == 0 else "gpdB", name=name + "_ld")
        cv = S.add("gpsimd", lambda e: e.tensor_copy(out=bb3[:, off:off + 256],
                                                     in_=fb3[:, off:off + 256]),
                   waits=[dw], name=name + "_cast")
        st = S.add("gpsimd", lambda e: st_fn(e, bb3, off), waits=[cv], dma=1,
                   sem="gpsA" if i % 2 == 0 else "gpsB", name=name + "_st")
        st_slot[i % 2] = st
        return st

    def job(ld_fn, st_fn, name):
        def mk(ch, q01):
            def ld2(e, fbuf, off, ch=ch, q01=q01):
                return ld_fn(e, fbuf, off, ch, q01)
            def st2(e, bbuf, off, ch=ch, q01=q01):
                return st_fn(e, bbuf, off, ch, q01)
            return ld2, st2
        for ch in range(2):
            for q01 in range(2):
                l2, s2 = mk(ch, q01)
                tok = job2(l2, s2, name)
        return tok

    for k in range(KC):
        for part in range(3):
            def ld(e, fbuf, off, ch, q01, k=k, part=part):
                cq = ch * 2 + q01
                return e.dma_start(out=fbuf[:, off:off + 256],
                                   in_=T["Wqkv"][k * 128:(k + 1) * 128,
                                                 part * 1024 + cq * 256:part * 1024 + (cq + 1) * 256])
            if part < 2:
                def st(e, bbuf, off, ch, q01, k=k, part=part):
                    cq = ch * 2 + q01
                    dst = T["s_wqkvQK"].rearrange("oc p c -> p oc c")[
                        :, part * 8 + cq * 2:part * 8 + (cq + 1) * 2, k * 128:(k + 1) * 128]
                    return [e.dma_start(out=dst, in_=bbuf[:, off:off + 256]
                                        .rearrange("p (oc c) -> p oc c", c=128))]
            else:
                def st(e, bbuf, off, ch, q01, k=k):
                    cq = ch * 2 + q01
                    return [e.dma_start(out=T["s_wqkvV"][k * 128:(k + 1) * 128,
                                                         cq * 256:(cq + 1) * 256],
                                        in_=bbuf[:, off:off + 256])]
            tok = job(ld, st, "qkv")
    ready["qkv"] = tok
    for k in range(KC):
        def ld(e, fbuf, off, ch, q01, k=k):
            cq = ch * 2 + q01
            return e.dma_start(out=fbuf[:, off:off + 256],
                               in_=T["Wout"][k * 128:(k + 1) * 128, cq * 256:(cq + 1) * 256])
        def st(e, bbuf, off, ch, q01, k=k):
            cq = ch * 2 + q01
            return [e.dma_start(out=T["s_wout"][k * 128:(k + 1) * 128, cq * 256:(cq + 1) * 256],
                                in_=bbuf[:, off:off + 256])]
        tok = job(ld, st, "wo")
    ready["wout"] = tok
    return ready


def _ln_stats(nc, S, G, tiles, prev, scr, coff=0):
    n = len(tiles)
    red = []
    sq = []
    for i, t in enumerate(tiles):
        ii = coff + i
        red.append(S.add("vector", lambda e, ii=ii, t=t:
                         e.reduce_sum(out=G["st_sum"][:, ii:ii + 1],
                                      in_=G["X"][:, t * D:(t + 1) * D], axis=AX.X),
                         waits=[prev], name="ln_red"))
        wsq = [prev]
        if i >= 1:
            wsq.append(sq[i - 1])
        elif G.get("_last_sq"):
            wsq.append(G["_last_sq"])
        sq.append(S.add("scalar", lambda e, ii=ii, t=t:
                        e.activation(out=scr[:, 0:D],
                                     in_=G["X"][:, t * D:(t + 1) * D],
                                     func=AF.Square, accum_out=G["st_sq"][:, ii:ii + 1]),
                        waits=wsq, name="ln_sq"))
    G["_last_sq"] = sq[-1]
    a, b = coff, coff + n
    w = [red[-1], sq[-1]]
    s1 = S.add("vector", lambda e: e.tensor_scalar_mul(G["st_mean"][:, a:b],
                                                       G["st_sum"][:, a:b], 1.0 / D),
               waits=w, name="ln_mean")
    s2 = S.add("vector", lambda e: e.tensor_mul(out=G["st_nmr"][:, a:b],
                                                in0=G["st_mean"][:, a:b],
                                                in1=G["st_mean"][:, a:b]),
               waits=[s1], name="ln_m2")
    s3 = S.add("vector", lambda e: e.tensor_scalar_mul(G["st_sq"][:, a:b],
                                                       G["st_sq"][:, a:b], 1.0 / D),
               waits=[s2], name="ln_msq")
    s4 = S.add("vector", lambda e: e.tensor_sub(out=G["st_sq"][:, a:b],
                                                in0=G["st_sq"][:, a:b],
                                                in1=G["st_nmr"][:, a:b]),
               waits=[s3], name="ln_var")
    s5 = S.add("scalar", lambda e: e.activation(out=G["st_rstd"][:, a:b],
                                                in_=G["st_sq"][:, a:b],
                                                func=AF.Sqrt, bias=G["eps"]),
               waits=[s4], name="ln_sqrt")
    s6 = S.add("vector", lambda e: e.reciprocal(out=G["st_rstd"][:, a:b],
                                                in_=G["st_rstd"][:, a:b]),
               waits=[s5], name="ln_recip")
    return s6


def _ln_apply(nc, S, G, tiles, stats_tok, xn_fm, xn_tm, W, prev, nslots=2,
              coff=0, sbase=0, prev_tr=(), pe_pt=None):
    tr_tok = {}
    app_tok = {}
    ev_hist = []
    for i, t in enumerate(tiles):
        w = [stats_tok, prev]
        if i >= nslots:
            w.append(tr_tok[i - nslots])
        elif prev_tr:
            w.append(prev_tr[i % len(prev_tr)])
        sl = (sbase + i % nslots) * D
        ii = coff + i
        app_tok[i] = S.add("vector", lambda e, ii=ii, t=t, sl=sl:
                           e.tensor_scalar(out=xn_tm[:, sl:sl + D],
                                           in0=G["X"][:, t * D:(t + 1) * D],
                                           scalar1=G["st_mean"][:, ii:ii + 1],
                                           scalar2=G["st_rstd"][:, ii:ii + 1],
                                           op0=ALU.subtract, op1=ALU.mult),
                           waits=w, name="ln_app")
        if pe_pt is not None:
            for u in range(KC):
                ti = i * KC + u
                pt = pe_pt[ti % 2]
                w2 = [app_tok[i]]
                if len(ev_hist) >= 2:
                    w2.append(ev_hist[-2])
                ptk = S.add("tensor", lambda e, sl=sl, u=u, pt=pt:
                            e.matmul(pt[:, :], xn_tm[:, sl + u * 128:sl + (u + 1) * 128],
                                     G["identb"][:, :], is_transpose=True,
                                     start=True, stop=True),
                            waits=w2, name="xn_tr")
                ev = S.add("scalar", lambda e, i=i, u=u, pt=pt:
                           e.copy(out=xn_fm[:, u * W + i * 128:u * W + (i + 1) * 128],
                                  in_=pt[:, :]),
                           waits=[ptk], name="xn_ev")
                ev_hist.append(ev)
            tr_tok[i] = ev_hist[-1]
            continue
        def trf(e, i=i, sl=sl):
            outs = []
            for k in range(KC):
                outs.append(e.dma_start(
                    out=xn_fm[:, k * W + i * 128:k * W + (i + 1) * 128],
                    in_=xn_tm[:, sl + k * 128:sl + (k + 1) * 128],
                    transpose=True))
            return outs
        tr_tok[i] = S.add("scalar" if i % 2 == 0 else "sync", trf,
                          waits=[app_tok[i]], dma=KC,
                          sem="trA" if i % 2 == 0 else "trB", name="ln_tr")
    return tr_tok


def _ffn(nc, S, T, G, es, tiles, tag, prev=None):
    ntile = len(tiles)
    W = ntile * 128
    nslab = (W + SLAB - 1) // SLAB

    es2 = contextlib.ExitStack()
    with es2:
        sb2 = lambda name, shape, dt=F32: es2.enter_context(nc.sbuf_tensor(tag + name, shape, dt))
        xn_fm = sb2("xn", [128, KC * W], BF16)
        xn_tm = sb2("xt", [128, D], BF16)
        h_buf = sb2("h", [128, 48 * SLAB], BF16)
        wbuf = sb2("w", [128, 2 * 4096], BF16)
        dbuf = sb2("d", [128, 4 * D], BF16)
        s_scr = sb2("s", [128, 2 * SLAB])
        scores = sb2("sc", [128, ntile * 16])
        r_sb = sb2("rs", [128, ntile * 16])
        r_g2 = sb2("g2", [128, ntile * 4])
        r_gs = sb2("gs", [128, ntile * 4])
        r_cnt = sb2("cn", [128, ntile * 4])
        r_msk = sb2("mk", [128, ntile * 16])
        r_msb = sb2("mb", [128, ntile * 16])
        r_m = G["st_nmr"][:, 0:ntile]
        r_sel = sb2("se", [128, ntile * 16])
        r_w = r_sb
        r_dn = r_g2[:, 0:ntile]
        c_all = r_msk
        G["_dbg_xn"] = xn_fm
        G["_dbg_sc"] = scores
        G["_dbg_ca"] = c_all
        ckm_b = sb2("ck", [16, W], BF16)

        pD0 = es2.enter_context(nc.psum_tensor(tag + "pD0", [128, D], F32))
        pD1 = es2.enter_context(nc.psum_tensor(tag + "pD1", [128, D], F32))
        es_pt = contextlib.ExitStack()
        pe_pt = None
        if tag == "f2":
            pe_pt = (es_pt.enter_context(nc.psum_tensor(tag + "ptA", [128, 128], BF16)),
                     es_pt.enter_context(nc.psum_tensor(tag + "ptB", [128, 128], BF16)))

        stats = _ln_stats(nc, S, G, tiles, prev, xn_tm)
        tr_tok = _ln_apply(nc, S, G, tiles, stats, xn_fm, xn_tm, W, prev, nslots=1,
                           pe_pt=pe_pt)

        sig_tok = {}
        log_tok = {}
        for i, t in enumerate(tiles):
            w = [tr_tok[i], G["_rw_ev"]]
            if i >= 1:
                w.append(sig_tok[i - 1])
            def lf(e, i=i):
                for k in range(KC):
                    ins = e.matmul(pD0[:, 0:16],
                                   xn_fm[:, k * W + i * 128:k * W + (i + 1) * 128],
                                   G["rwT"][:, k * 16:(k + 1) * 16],
                                   start=(k == 0), stop=(k == KC - 1))
                return ins
            log_tok[i] = S.add("tensor", lf, waits=w, name="r_log")
            sig_tok[i] = S.add("scalar", lambda e, i=i:
                               e.activation(out=scores[:, i * 16:(i + 1) * 16],
                                            in_=pD0[:, 0:16], func=AF.Sigmoid),
                               waits=[log_tok[i]], name="r_sig")

        c_tok = _routing(nc, S, G, ntile, scores, r_sb, r_g2, r_gs, r_cnt, r_msk,
                         r_msb, r_m, r_sel, r_w, r_dn, c_all,
                         sig_tok[ntile - 1])

        ck_ev = {}
        ldc_tot = S.total("ldc")
        for i in range(ntile):
            w = [c_tok, ldc_tot]
            if i >= 2:
                w.append(ck_ev[i - 2])
            ptk = S.add("tensor", lambda e, i=i:
                        e.matmul(pD1[0:16, (i % 2) * 512:(i % 2) * 512 + 128],
                                 c_all[:, i * 16:(i + 1) * 16],
                                 G["identf"][:, :], is_transpose=True, start=True, stop=True),
                        waits=w, name="c_tr")
            ck_ev[i] = S.add("scalar", lambda e, i=i:
                             e.copy(out=ckm_b[0:16, i * 128:(i + 1) * 128],
                                    in_=pD1[0:16, (i % 2) * 512:(i % 2) * 512 + 128]),
                             waits=[ptk], name="c_ev")
        ck_done = ck_ev[ntile - 1]
        es_pt.close()

        pG0 = es2.enter_context(nc.psum_tensor(tag + "pG0", [128, SLAB], F32))
        pU0 = es2.enter_context(nc.psum_tensor(tag + "pU0", [128, SLAB], F32))
        pG1 = es2.enter_context(nc.psum_tensor(tag + "pG1", [128, SLAB], F32))
        pU1 = es2.enter_context(nc.psum_tensor(tag + "pU1", [128, SLAB], F32))

        phaseB_last = (ck_done, log_tok[ntile - 1], None, None)
        for s in range(nslab):
            tok0 = s * SLAB
            stoks = min(SLAB, W - tok0)
            stc = stoks // 128
            phaseB_last = _ffn_slab(nc, S, T, G, tiles, s, tok0, stoks, stc, W,
                                    xn_fm, h_buf, wbuf, dbuf, s_scr, ckm_b,
                                    pG0, pU0, pG1, pU1, pD0, pD1,
                                    ck_done, tr_tok, phaseB_last)
        return phaseB_last[0]


def _routing(nc, S, G, ntile, scores, r_sb, r_g2, r_gs, r_cnt, r_msk, r_msb,
             r_m, r_sel, r_w, r_dn, c_all, prev):
    n16 = ntile * 16
    n4 = ntile * 4
    tok = prev
    lds_w = [S.total("lds")]

    def V(fn, name):
        nonlocal tok
        tok = S.add("vector", fn, waits=[tok], name=name)
        return tok

    for t in range(ntile):
        tok = S.add("vector", lambda e, t=t: e.tensor_add(out=r_sb[:, t * 16:(t + 1) * 16],
                                                          in0=scores[:, t * 16:(t + 1) * 16],
                                                          in1=G["cb"][:, :]),
                    waits=[tok] + lds_w, name="rt_sb")
    sbg = r_sb[:, 0:n16].rearrange("p (tg e) -> p tg e", e=4)
    g2v = r_g2[:, 0:n4]
    gsv = r_gs[:, 0:n4]
    pairs = [(0, 1), (0, 2), (0, 3), (1, 2), (1, 3), (2, 3)]
    for pi, (i, j) in enumerate(pairs):
        V(lambda e, i=i, j=j: e.tensor_add(out=g2v, in0=sbg[:, :, i], in1=sbg[:, :, j]), "rt_pair")
        if pi == 0:
            V(lambda e: e.tensor_copy(out=gsv, in_=g2v), "rt_cp")
        else:
            V(lambda e: e.tensor_max(out=gsv, in0=gsv, in1=g2v), "rt_max")
    V(lambda e: e.memset(r_cnt[:, 0:n4], 0.0), "rt_z")
    gst = r_gs[:, 0:n4].rearrange("p (t g) -> p t g", g=4)
    g2t = r_g2[:, 0:n4].rearrange("p (t g) -> p t g", g=4)
    for sft in range(1, 4):
        m = 4 - sft
        V(lambda e, m=m, sft=sft: e.tensor_tensor(out=g2t[:, :, 0:m], in0=gst[:, :, 0:m],
                                                  in1=gst[:, :, sft:4], op=ALU.is_lt), "rt_c1")
        V(lambda e, m=m, sft=sft: e.tensor_tensor(out=g2t[:, :, m:4], in0=gst[:, :, m:4],
                                                  in1=gst[:, :, 0:sft], op=ALU.is_lt), "rt_c2")
        V(lambda e: e.tensor_add(out=r_cnt[:, 0:n4], in0=r_cnt[:, 0:n4], in1=g2v), "rt_acc")
    V(lambda e: e.tensor_single_scalar(out=r_cnt[:, 0:n4], in_=r_cnt[:, 0:n4],
                                       scalar=2.0, op=ALU.is_lt), "rt_gm")
    mskv = r_msk[:, 0:n16].rearrange("p (tg e) -> p tg e", e=4)
    for e4 in range(4):
        V(lambda e, e4=e4: e.tensor_copy(out=mskv[:, :, e4], in_=r_cnt[:, 0:n4]), "rt_exp")
    V(lambda e: e.tensor_scalar(out=r_msb[:, 0:n16], in0=r_msk[:, 0:n16],
                                scalar1=1e9, scalar2=-1e9, op0=ALU.mult, op1=ALU.add), "rt_pen")
    V(lambda e: e.tensor_mul(out=r_w[:, 0:n16], in0=r_sb[:, 0:n16], in1=r_msk[:, 0:n16]), "rt_mul")
    V(lambda e: e.tensor_add(out=r_msb[:, 0:n16], in0=r_msb[:, 0:n16], in1=r_w[:, 0:n16]), "rt_msb")
    V(lambda e: e.memset(r_sel[:, 0:n16], 0.0), "rt_zs")
    msbv = r_msb[:, 0:n16].rearrange("p (t e) -> p t e", e=16)
    for it in range(4):
        V(lambda e: e.tensor_reduce(out=r_m[:, 0:ntile], in_=msbv, axis=AX.X, op=ALU.max),
          "rt_rm")
        for t in range(ntile):
            V(lambda e, t=t: e.tensor_scalar(out=r_w[:, t * 16:(t + 1) * 16],
                                             in0=r_msb[:, t * 16:(t + 1) * 16],
                                             scalar1=r_m[:, t:t + 1], scalar2=None,
                                             op0=ALU.is_ge), "rt_ge")
        V(lambda e: e.tensor_add(out=r_sel[:, 0:n16], in0=r_sel[:, 0:n16],
                                 in1=r_w[:, 0:n16]), "rt_sel")
        if it < 3:
            V(lambda e: e.scalar_tensor_tensor(out=r_msb[:, 0:n16], in0=r_w[:, 0:n16],
                                               scalar=-2e9, in1=r_msb[:, 0:n16],
                                               op0=ALU.mult, op1=ALU.add), "rt_rem")
    V(lambda e: e.tensor_mul(out=r_w[:, 0:n16], in0=scores[:, 0:n16],
                             in1=r_sel[:, 0:n16]), "rt_w")
    V(lambda e: e.tensor_reduce(out=r_dn[:, 0:ntile],
                                in_=r_w[:, 0:n16].rearrange("p (t e) -> p t e", e=16),
                                axis=AX.X, op=ALU.add), "rt_dn")
    V(lambda e: e.tensor_scalar_add(r_dn[:, 0:ntile], r_dn[:, 0:ntile], 1e-20), "rt_eps")
    V(lambda e: e.reciprocal(out=r_dn[:, 0:ntile], in_=r_dn[:, 0:ntile]), "rt_rc")
    for t in range(ntile):
        V(lambda e, t=t: e.tensor_scalar(out=c_all[:, t * 16:(t + 1) * 16],
                                         in0=r_w[:, t * 16:(t + 1) * 16],
                                         scalar1=r_dn[:, t:t + 1], scalar2=None,
                                         op0=ALU.mult), "rt_c")
    return tok


def _ffn_slab(nc, S, T, G, tiles, s, tok0, stoks, stc, W,
              xn_fm, h_buf, wbuf, dbuf, s_scr, ckm_b,
              pG0, pU0, pG1, pU1, pD0, pD1, ck_done, tr_tok, phaseB_prev):
    X = G["X"]
    RD = G["_ready"]
    pBtok, pB_pD_free, wb_free, db_free = phaseB_prev

    dw_w = {}
    pe_gu = {}
    pe_gu_g = {}
    pe_c = {}
    ac_silu = {}
    dv_h = {}
    RD = G["_ready"]

    def emit_wld(it, w):
        w = list(w) + [RD["geu"][it] if it < NE else RD["shared"][it - NE]]
        dw_w[it] = S.add("sync", lambda e, it=it:
                         [e.dma_start(out=wbuf[:, (it % 2) * 4096:(it % 2 + 1) * 4096],
                                      in_=T["s_geu"][it, :, :])],
                         waits=w, dma=1, sem="wbA" if it % 2 == 0 else "wbB",
                         name="gu_ld")
    emit_wld(0, [wb_free or pBtok])
    emit_wld(1, [wb_free or pBtok])
    for it in range(NIT):
        routed = it < NE
        wb = (it % 2) * 4096
        pC = pD0 if it % 2 == 0 else pD1
        pCs = lambda p: p[:, 0:stoks]
        if routed:
            w = [S.total("ldc"), ck_done]
            if it >= 2:
                w.append(dv_h.get((it - 2, 1)))
            else:
                w.append(pBtok)
                w.append(pB_pD_free)
            pe_c[it] = S.add("tensor", lambda e, it=it, pC=pC:
                             e.matmul(pC[:, 0:stoks],
                                      G["onehot"][0:16, it * 128:(it + 1) * 128],
                                      ckm_b[0:16, tok0:tok0 + stoks],
                                      start=True, stop=True),
                             waits=w, name="c_mm")
        for half in range(2):
            pG = pG0 if half == 0 else pG1
            pU = pU0 if half == 0 else pU1
            w = [dw_w[it]]
            if it >= 1:
                w.append(dv_h.get((it - 1, half)))
            elif s == 0:
                w.append(tr_tok[len(tiles) - 1])
                w.append(tr_tok[len(tiles) - 2])
            def gf(e, it=it, wb=wb, half=half, pG=pG):
                for k in range(KC):
                    ins = e.matmul(pG[:, 0:stoks],
                                   wbuf[:, wb + k * 512 + half * 128:wb + k * 512 + half * 128 + 128],
                                   xn_fm[:, k * W + tok0:k * W + tok0 + stoks],
                                   start=(k == 0), stop=(k == KC - 1))
                return ins
            def uf(e, it=it, wb=wb, half=half, pU=pU):
                for k in range(KC):
                    ins = e.matmul(pU[:, 0:stoks],
                                   wbuf[:, wb + k * 512 + 256 + half * 128:wb + k * 512 + 256 + half * 128 + 128],
                                   xn_fm[:, k * W + tok0:k * W + tok0 + stoks],
                                   start=(k == 0), stop=(k == KC - 1))
                return ins
            pe_g = S.add("tensor", gf, waits=w, name="g_mm")
            pe_gu[(it, half)] = S.add("tensor", uf, waits=[], name="u_mm")
            pe_gu_g[(it, half)] = pe_g

            w = [pe_gu_g[(it, half)], pe_gu[(it, half)]]
            if it >= 1:
                w.append(dv_h.get((it - 1, half)))
            ac_silu[(it, half)] = S.add("scalar", lambda e, half=half, pG=pG:
                                        e.activation(out=s_scr[:, half * SLAB:half * SLAB + stoks],
                                                     in_=pG[:, 0:stoks], func=AF.Silu),
                                        waits=w, name="silu")
            chunk = 2 * it + half
            hdst = h_buf[:, chunk * SLAB:chunk * SLAB + stoks]
            if routed:
                d1 = S.add("vector", lambda e, half=half, pU=pU:
                           e.tensor_mul(out=s_scr[:, half * SLAB:half * SLAB + stoks],
                                        in0=s_scr[:, half * SLAB:half * SLAB + stoks],
                                        in1=pU[:, 0:stoks]),
                           waits=[ac_silu[(it, half)]], name="hmul1")
                dv_h[(it, half)] = S.add("vector", lambda e, hdst=hdst, pC=pC, half=half:
                                         e.tensor_mul(out=hdst,
                                                      in0=s_scr[:, half * SLAB:half * SLAB + stoks],
                                                      in1=pC[:, 0:stoks]),
                                         waits=[d1, pe_c[it]], name="hmul2")
            else:
                dv_h[(it, half)] = S.add("vector", lambda e, half=half, pU=pU, hdst=hdst:
                                         e.tensor_mul(out=hdst,
                                                      in0=s_scr[:, half * SLAB:half * SLAB + stoks],
                                                      in1=pU[:, 0:stoks]),
                                         waits=[ac_silu[(it, half)]], name="hmul")
        if it + 2 < NIT:
            emit_wld(it + 2, [pe_gu[(it, 1)]])

    npass = (stc + 1) // 2
    lastX = None
    pD_free = None
    for p in range(npass):
        tcs = [tc for tc in (2 * p, 2 * p + 1) if tc < stc]
        dw_d = {}
        pe_d = {}

        def emit_dld(c, w):
            w = list(w) + [RD["down"][c] if c < 32 else RD["sd"][c - 32],
                           RD["down"][c + 1] if c + 1 < 32 else RD["sd"][c + 1 - 32]]
            tok = S.add("sync" if c % 4 == 0 else "scalar", lambda e, c=c:
                        [e.dma_start(out=dbuf[:, (c % 4) * D:(c % 4 + 2) * D]
                                     .rearrange("p (two c) -> p two c", c=D),
                                     in_=T["s_down"][c * 128:(c + 2) * 128, :]
                                     .rearrange("(two p) c -> p two c", p=128))],
                        waits=w, dma=1, sem="dbA" if c % 4 == 0 else "dbB",
                        name="d_ld")
            dw_d[c] = tok
            dw_d[c + 1] = tok
        if p >= 1:
            w0a, w0b = prev_pe
        elif db_free is not None:
            w0a, w0b = db_free
        else:
            w0a = w0b = pBtok
        emit_dld(0, [w0a])
        emit_dld(2, [w0b])
        for c in range(48):
            w = [dw_d[c], dv_h[(c // 2, c % 2)]]
            if c == 0:
                w.append(dv_h.get((NIT - 1, 1)))
                if p >= 1:
                    w.append(lastX)
                else:
                    w.append(pB_pD_free)
            def df(e, c=c, tcs=tcs):
                for ti, tc in enumerate(tcs):
                    pD = pD0 if ti == 0 else pD1
                    for nh in range(2):
                        ins = e.matmul(pD[:, nh * 512:(nh + 1) * 512],
                                       h_buf[:, c * SLAB + tc * 128:c * SLAB + tc * 128 + 128],
                                       dbuf[:, (c % 4) * D + nh * 512:(c % 4) * D + (nh + 1) * 512],
                                       start=(c == 0), stop=(c == 47))
                return ins
            pe_d[c] = S.add("tensor", df, waits=w, name="d_mm")
            if c % 2 == 1 and c + 3 < 48:
                emit_dld(c + 3, [pe_d[c]])
        xts = []
        for ti, tc in enumerate(tcs):
            pD = pD0 if ti == 0 else pD1
            t = tiles[(tok0 + tc * 128) // 128]
            xts.append(S.add("vector", lambda e, pD=pD, t=t:
                             e.scalar_tensor_tensor(out=X[:, t * D:(t + 1) * D],
                                                    in0=pD[:, 0:D], scalar=0.5,
                                                    in1=X[:, t * D:(t + 1) * D],
                                                    op0=ALU.mult, op1=ALU.add),
                             waits=[pe_d[47]], name="xupd"))
        lastX = xts[-1]
        pD_free = xts[-1]
        prev_pe = (pe_d[45], pe_d[47])
    return (lastX, pD_free, pe_gu[(NIT - 1, 1)], prev_pe)


pB_last_ped = {}


def _attention(nc, S, T, G, es, prev):
    pool_tot = [S.total("gpsA"), S.total("gpsB"), S.total("gpdA"), S.total("gpdB")]
    for _e in ("vector", "scalar", "sync", "gpsimd"):
        S.fence(_e, pool_tot, name="pool_gate")
    es2 = contextlib.ExitStack()
    with es2:
        sb2 = lambda name, shape, dt=F32: es2.enter_context(nc.sbuf_tensor("a" + name, shape, dt))
        K_fm = sb2("K", [128, KC * NTOK], BF16)
        V_tm = sb2("V", [128, NT * 1040], BF16)
        xn_fm = sb2("xn", [128, 2 * KC * 512], BF16)
        xn_tm = sb2("xt", [128, 4 * D], BF16)

        groups = [list(range(0, 4)), list(range(4, 8)), list(range(8, 12)),
                  list(range(12, 16)), list(range(16, 18))]

        qkv_done = []
        with contextlib.ExitStack() as es3:
            wv_sb = es3.enter_context(nc.sbuf_tensor("awv", [128, KC * D], BF16))
            wq_sb = es3.enter_context(nc.sbuf_tensor("awq", [128, 2 * 1024], BF16))
            qstage = es3.enter_context(nc.sbuf_tensor("aqs", [128, 2 * 512], BF16))
            pQK = es3.enter_context(nc.psum_tensor("apQK", [128, 512], F32))
            pQK2 = es3.enter_context(nc.psum_tensor("apQK2", [128, 512], F32))
            pV = es3.enter_context(nc.psum_tensor("apV", [128, D], F32))

            wv_tok = S.add("sync", lambda e: [
                e.dma_start(out=wv_sb[:, k * D:(k + 1) * D],
                            in_=T["s_wqkvV"][k * 128:(k + 1) * 128, :]) for k in range(KC)],
                waits=[prev, G["_ready"]["qkv"]], dma=KC, sem="wv", name="wv_ld")

            carry = {"done": {-2: prev, -1: prev}, "wq": None, "pqk_ev": None,
                     "sp": None, "vev": None}
            for gi, gtiles in enumerate(groups):
                xn_view = xn_fm[:, (gi % 2) * KC * 512:(gi % 2) * KC * 512 + KC * 512]
                asq = xn_tm[:, (gi % 2) * 2 * D:(gi % 2) * 2 * D + D]
                _qkv_group(nc, S, T, G, gi, gtiles, xn_view, xn_tm, asq, wq_sb,
                           wv_sb, qstage, K_fm, V_tm, pQK, pQK2, pV,
                           wv_tok, carry)
            qkv_done = carry["done"][len(groups) - 1]
            S.fence("scalar", [S.total("spA"), S.total("spB")], name="sp_fence")
        S.new_block()

        with contextlib.ExitStack() as es4:
            sb4 = lambda name, shape, dt=F32: es4.enter_context(nc.sbuf_tensor("b" + name, shape, dt))
            wo_sb = sb4("wo", [128, KC * D], BF16)
            qring = sb4("qr", [128, 2 * 1024], BF16)
            exps = sb4("ex", [128, 2 * 384], BF16)
            att_fm = sb4("ao", [128, 2 * 1024], BF16)
            t_av = sb4("tv", [128, 128])
            rrow = sb4("rw", [64, 128])
            rrowb = sb4("rb", [64, 128], BF16)
            pS = es4.enter_context(nc.psum_tensor("bpS", [128, 384], F32))
            pS2 = es4.enter_context(nc.psum_tensor("bpS2", [128, 384], F32))
            pAV = es4.enter_context(nc.psum_tensor("bpAV", [128, 128], F32))
            pNR = es4.enter_context(nc.psum_tensor("bpNR", [128, 256], F32))
            pO = es4.enter_context(nc.psum_tensor("bpO", [128, D], F32))

            wo_tok = S.add("sync", lambda e: [
                e.dma_start(out=wo_sb[:, k * D:(k + 1) * D],
                            in_=T["s_wout"][k * 128:(k + 1) * 128, :]) for k in range(KC)],
                waits=[qkv_done, G["_ready"]["wout"]], dma=KC, sem="wo", name="wo_ld")

            last = _qb_loop(nc, S, T, G, K_fm, V_tm, qring, exps, att_fm, t_av,
                            rrow, rrowb, wo_sb, pS, pS2, pAV, pNR, pO,
                            wo_tok, qkv_done)
        return last


def _qkv_group(nc, S, T, G, gi, gtiles, xn_fm, xn_tm, asq, wq_sb, wv_sb, qstage,
               K_fm, V_tm, pQK, pQK2, pV, wv_tok, carry):
    t0 = gtiles[0]
    gn = len(gtiles)
    gw = gn * 128

    prev = carry["done"][gi - 2]
    stats = _ln_stats(nc, S, G, gtiles, prev, asq, coff=t0)
    tr_tok = _ln_apply(nc, S, G, gtiles, stats, xn_fm, xn_tm, gw, prev,
                       nslots=2, coff=t0, sbase=(gi % 2) * 2,
                       prev_tr=carry.get("tr") or ())
    carry["tr"] = (tr_tok[max(0, gn - 2)], tr_tok[gn - 1])
    xn_ready = (tr_tok[gn - 1], tr_tok[max(0, gn - 2)])

    dw_q = {}
    pe_qk = {}
    ev_qk = {}
    sp_q = {}

    def emit_qkld(oc, w):
        dw_q[oc] = S.add("sync", lambda e, oc=oc:
                         [e.dma_start(out=wq_sb[:, (oc % 2) * 1024:(oc % 2 + 1) * 1024],
                                      in_=T["s_wqkvQK"][oc, :, :])],
                         waits=w, dma=1, sem="qkA" if oc % 2 == 0 else "qkB",
                         name="qk_ld")
    qkv_rd = G["_ready"]["qkv"]
    w0 = [qkv_rd]
    w1 = [qkv_rd]
    if carry["wq"]:
        w0.append(carry["wq"][0])
        w1.append(carry["wq"][1])
    else:
        w0.append(prev)
        w1.append(prev)
    emit_qkld(0, w0)
    emit_qkld(1, w1)
    for oc in range(16):
        p = pQK if oc % 2 == 0 else pQK2
        w = [dw_q[oc]] + list(xn_ready)
        if oc >= 2:
            w.append(ev_qk[oc - 2])
        elif carry["pqk_ev"]:
            w.append(carry["pqk_ev"][oc])
        def qf(e, oc=oc, p=p, gw=gw):
            for k in range(KC):
                ins = e.matmul(p[:, 0:gw],
                               wq_sb[:, (oc % 2) * 1024 + k * 128:(oc % 2) * 1024 + (k + 1) * 128],
                               xn_fm[:, k * gw:(k + 1) * gw],
                               start=(k == 0), stop=(k == KC - 1))
            return ins
        pe_qk[oc] = S.add("tensor", qf, waits=w, name="qk_mm")
        if oc + 2 < 16:
            emit_qkld(oc + 2, [pe_qk[oc]])
        if oc < 8:
            w2 = [pe_qk[oc]]
            if oc >= 2:
                w2.append(sp_q[oc - 2])
            elif carry["sp"]:
                w2.append(carry["sp"][oc])
            ev_qk[oc] = S.add("vector", lambda e, oc=oc, p=p, gw=gw:
                              e.tensor_scalar(out=qstage[:, (oc % 2) * 512:(oc % 2) * 512 + gw],
                                              in0=p[:, 0:gw],
                                              scalar1=G["bqk"][:, oc:oc + 1], scalar2=None,
                                              op0=ALU.add),
                              waits=w2, name="q_ev")
            own_lo = max(t0, 1)
            own_hi = min(gtiles[-1], 16)
            if own_hi >= own_lo:
                def spf(e, oc=oc, own_lo=own_lo, own_hi=own_hi, t0=t0):
                    return [e.dma_start(
                        out=T["q_scr"][oc, :, (own_lo - 1) * 128:own_hi * 128],
                        in_=qstage[:, (oc % 2) * 512 + (own_lo - t0) * 128:
                                   (oc % 2) * 512 + (own_hi - t0 + 1) * 128])]
                sp_q[oc] = S.add("scalar", spf, waits=[ev_qk[oc]], dma=1,
                                 sem="spA" if oc % 2 == 0 else "spB", name="q_sp")
            else:
                sp_q[oc] = ev_qk[oc]
        else:
            ev_qk[oc] = S.add("vector", lambda e, oc=oc, p=p, gw=gw, t0=t0:
                              e.tensor_scalar(out=K_fm[:, (oc - 8) * NTOK + t0 * 128:
                                                      (oc - 8) * NTOK + t0 * 128 + gw],
                                              in0=p[:, 0:gw],
                                              scalar1=G["bqk"][:, oc:oc + 1], scalar2=None,
                                              op0=ALU.add),
                              waits=[pe_qk[oc]], name="k_ev")

    pe_v = {}
    ev_v = {}
    for i, t in enumerate(gtiles):
        w = [wv_tok, tr_tok[i], pe_qk[15]]
        if i >= 1:
            w.append(ev_v[i - 1])
        elif carry["vev"]:
            w.append(carry["vev"])
        def vf(e, i=i, gw=gw):
            for k in range(KC):
                e.matmul(pV[:, 0:512],
                         xn_fm[:, k * gw + i * 128:k * gw + (i + 1) * 128],
                         wv_sb[:, k * D:k * D + 512], start=(k == 0), stop=False)
                e.matmul(pV[:, 512:1024],
                         xn_fm[:, k * gw + i * 128:k * gw + (i + 1) * 128],
                         wv_sb[:, k * D + 512:(k + 1) * D], start=(k == 0), stop=False)
            e.matmul(pV[:, 0:512], G["onesb"][0:1, :], G["bvb"][0:1, 0:512],
                     start=False, stop=True)
            ins = e.matmul(pV[:, 512:1024], G["onesb"][0:1, :], G["bvb"][0:1, 512:1024],
                           start=False, stop=True)
            return ins
        pe_v[i] = S.add("tensor", vf, waits=w, name="v_mm")
        m1 = S.add("vector", lambda e, t=t: e.memset(V_tm[:, t * 1040:(t + 1) * 1040], 1.0),
                   waits=[], name="v_ones")
        def vev(e, t=t):
            for h in range(NH):
                ins = e.tensor_copy(out=V_tm[:, t * 1040 + h * 65:t * 1040 + h * 65 + 64],
                                    in_=pV[:, h * 64:(h + 1) * 64])
            return ins
        ev_v[i] = S.add("vector", vev, waits=[pe_v[i], m1], name="v_ev")

    carry["wq"] = (pe_qk[14], pe_qk[15])
    carry["pqk_ev"] = (ev_qk[14], ev_qk[15])
    carry["sp"] = (sp_q[6], sp_q[7])
    carry["vev"] = ev_v[gn - 1]
    carry["done"][gi] = ev_v[gn - 1]


def _qb_loop(nc, S, T, G, K_fm, V_tm, qring, exps, att_fm, t_av, rrow, rrowb,
             wo_sb, pS, pS2, pAV, pNR, pO, wo_tok, qkv_done):
    dw_qr = {}
    sc_pe = {}
    exp_ac = {}
    msk_dv = {}
    av_pe = {}
    nm_pe = {}
    rc_dv = {}
    rb_ac = {}
    rb_pe = {}
    avev_ac = {}
    ao_dv = {}
    opj_pe = {}
    xu_dv = {}

    def emit_qld(j, w):
        dw_qr[j] = S.add("sync", lambda e, j=j:
                         [e.dma_start(out=qring[:, (j % 2) * 1024:(j % 2 + 1) * 1024]
                                      .rearrange("p (oc c) -> p oc c", c=128),
                                      in_=T["q_scr"].rearrange("oc p c -> p oc c")
                                      [:, :, (j - 1) * 128:j * 128])],
                         waits=w, dma=1, sem="qrA" if j % 2 == 0 else "qrB",
                         name="q_ld")
    spills = [S.total("spA"), S.total("spB")]
    emit_qld(1, [qkv_done] + spills)
    emit_qld(2, [qkv_done] + spills)

    for j in OWN:
        qb = (j % 2) * 1024
        for oc in range(8):
            for hi in range(2):
                pSc = pS if hi == 0 else pS2
                w = [dw_qr[j]]
                pk = (j, oc - 1, hi) if oc >= 1 else (j - 1, 7, hi)
                if exp_ac.get(pk):
                    w.append(exp_ac[pk])
                def scf(e, j=j, oc=oc, hi=hi, pSc=pSc, qb=qb):
                    for m, kb in enumerate((j - 1, j, j + 1)):
                        ins = e.matmul(
                            pSc[:, m * 128:(m + 1) * 128],
                            K_fm[hi * 64:(hi + 1) * 64,
                                 oc * NTOK + kb * 128:oc * NTOK + (kb + 1) * 128],
                            qring[hi * 64:(hi + 1) * 64, qb + oc * 128:qb + (oc + 1) * 128],
                            start=True, stop=True)
                    return ins
                sc_pe[(j, oc, hi)] = S.add("tensor", scf, waits=w, name="sc_mm")
                w = [sc_pe[(j, oc, hi)]]
                pk = (j, oc - 1, hi) if oc >= 1 else (j - 1, 7, hi)
                if av_pe.get(pk):
                    w.append(av_pe[pk])
                exp_ac[(j, oc, hi)] = S.add("scalar", lambda e, hi=hi, pSc=pSc:
                                            e.activation(out=exps[:, hi * 384:hi * 384 + 384],
                                                         in_=pSc[:, 0:384], func=AF.Exp,
                                                         scale=G["s32"]),
                                            waits=w, name="exp")
                mL = G["mLe"] if j == 1 else G["triL"]
                mR = G["mRe"] if j == 16 else G["triR"]
                d1 = S.add("vector", lambda e, hi=hi, mL=mL:
                           e.tensor_mul(out=exps[:, hi * 384:hi * 384 + 128],
                                        in0=exps[:, hi * 384:hi * 384 + 128], in1=mL[:, :]),
                           waits=[exp_ac[(j, oc, hi)]], name="mskL")
                msk_dv[(j, oc, hi)] = S.add("vector", lambda e, hi=hi, mR=mR:
                                            e.tensor_mul(out=exps[:, hi * 384 + 256:hi * 384 + 384],
                                                         in0=exps[:, hi * 384 + 256:hi * 384 + 384],
                                                         in1=mR[:, :]),
                                            waits=[d1], name="mskR")
            for hi in range(2):
                h = 2 * oc + hi
                w = [msk_dv[(j, oc, hi)]]
                pk = (j, oc - 1) if oc >= 1 else ((j - 1, 7) if j >= 2 else None)
                if pk and avev_ac.get(pk):
                    w.append(avev_ac[pk])
                def avf(e, j=j, h=h, hi=hi):
                    for m, kb in enumerate((j - 1, j, j + 1)):
                        ins = e.matmul(
                            pAV[hi * 64:(hi + 1) * 64, :],
                            V_tm[:, kb * 1040 + h * 65:kb * 1040 + h * 65 + 64],
                            exps[:, hi * 384 + m * 128:hi * 384 + (m + 1) * 128],
                            start=(m == 0), stop=(m == 2),
                            tile_position=(0, hi * 64))
                    return ins
                av_pe[(j, oc, hi)] = S.add("tensor", avf, waits=w, name="av_mm")
                w = [msk_dv[(j, oc, hi)]]
                pk = (j, oc - 1) if oc >= 1 else ((j - 1, 7) if j >= 2 else None)
                if pk and rc_dv.get(pk):
                    w.append(rc_dv[pk])
                if pk and ao_dv.get(pk):
                    w.append(ao_dv[pk])
                def nmf(e, j=j, h=h, hi=hi):
                    for m, kb in enumerate((j - 1, j, j + 1)):
                        ins = e.matmul(
                            pNR[hi * 32:hi * 32 + 1, 0:128],
                            V_tm[:, kb * 1040 + h * 65 + 64:kb * 1040 + h * 65 + 65],
                            exps[:, hi * 384 + m * 128:hi * 384 + (m + 1) * 128],
                            start=(m == 0), stop=(m == 2),
                            tile_position=(0, hi * 32))
                    return ins
                nm_pe[(j, oc, hi)] = S.add("tensor", nmf, waits=w, name="nm_mm")
            def rcf(e):
                e.reciprocal(out=rrow[0:1, 0:128], in_=pNR[0:1, 0:128])
                return e.reciprocal(out=rrow[32:33, 0:128], in_=pNR[32:33, 0:128])
            w = [nm_pe[(j, oc, 0)], nm_pe[(j, oc, 1)]]
            pk = (j, oc - 1) if oc >= 1 else ((j - 1, 7) if j >= 2 else None)
            if pk and rb_ac.get(pk):
                w.append(rb_ac[pk])
            rc_dv[(j, oc)] = S.add("vector", rcf, waits=w, name="recip")
            def rbf(e):
                e.copy(out=rrowb[0:1, 0:128], in_=rrow[0:1, 0:128])
                return e.copy(out=rrowb[32:33, 0:128], in_=rrow[32:33, 0:128])
            w = [rc_dv[(j, oc)]]
            if pk and rb_pe.get(pk):
                w.append(rb_pe[pk])
            rb_ac[(j, oc)] = S.add("scalar", rbf, waits=w, name="rcast")
            def rbm(e):
                e.matmul(pNR[0:64, 128:256], G["onesb"][0:1, 0:64], rrowb[0:1, 0:128],
                         start=True, stop=True)
                return e.matmul(pNR[64:128, 128:256], G["onesb"][32:33, 0:64],
                                rrowb[32:33, 0:128], start=True, stop=True,
                                tile_position=(32, 64))
            w = [rb_ac[(j, oc)]]
            pk2 = (j, oc - 1) if oc >= 1 else ((j - 1, 7) if j >= 2 else None)
            if pk2 and ao_dv.get(pk2):
                w.append(ao_dv[pk2])
            rb_pe[(j, oc)] = S.add("tensor", rbm, waits=w, name="rbc_mm")
            w = [av_pe[(j, oc, 0)], av_pe[(j, oc, 1)]]
            pk2 = (j, oc - 1) if oc >= 1 else ((j - 1, 7) if j >= 2 else None)
            if pk2 and ao_dv.get(pk2):
                w.append(ao_dv[pk2])
            avev_ac[(j, oc)] = S.add("scalar", lambda e: e.copy(out=t_av[:, :], in_=pAV[:, :]),
                                     waits=w, name="av_ev")
            w = [avev_ac[(j, oc)], rb_pe[(j, oc)]]
            if j >= 3:
                w.append(opj_pe.get(j - 2))
            ao_dv[(j, oc)] = S.add("vector", lambda e, j=j, oc=oc:
                                   e.tensor_mul(out=att_fm[:, (j % 2) * 1024 + oc * 128:
                                                          (j % 2) * 1024 + (oc + 1) * 128],
                                                in0=t_av[:, :], in1=pNR[:, 128:256]),
                                   waits=w, name="norm")
        if j + 2 <= 16:
            emit_qld(j + 2, [sc_pe[(j, 7, 1)]])
        w = [wo_tok, ao_dv[(j, 7)]]
        if j >= 2:
            w.append(xu_dv[j - 1])
        def opf(e, j=j):
            for k in range(KC):
                for nh in range(2):
                    e.matmul(pO[:, nh * 512:(nh + 1) * 512],
                             att_fm[:, (j % 2) * 1024 + k * 128:(j % 2) * 1024 + (k + 1) * 128],
                             wo_sb[:, k * D + nh * 512:k * D + (nh + 1) * 512],
                             start=(k == 0), stop=False)
            e.matmul(pO[:, 0:512], G["onesb"][0:1, :], G["bob"][0:1, 0:512],
                     start=False, stop=True)
            return e.matmul(pO[:, 512:1024], G["onesb"][0:1, :], G["bob"][0:1, 512:1024],
                            start=False, stop=True)
        opj_pe[j] = S.add("tensor", opf, waits=w, name="oproj")
        xu_dv[j] = S.add("vector", lambda e, j=j:
                         e.tensor_add(out=G["X"][:, j * D:(j + 1) * D],
                                      in0=pO[:, 0:D], in1=G["X"][:, j * D:(j + 1) * D]),
                         waits=[opj_pe[j]], name="xatt")
    return xu_dv[16]


def _host_constants():
    bf = ml_dtypes.bfloat16
    identb = np.eye(128, dtype=bf)
    identf = np.eye(128, dtype=np.float32)
    onehot = np.zeros((16, 2048), dtype=bf)
    for e in range(16):
        onehot[e, e * 128:(e + 1) * 128] = 1
    onesb = np.ones((128, 128), dtype=bf)
    i = np.arange(128)[:, None]
    j = np.arange(128)[None, :]
    triL = (i >= j).astype(bf)
    triR = (i <= j).astype(bf)
    return identb, identf, onehot, onesb, triL, triR


def _in_maps(inputs):
    x = np.ascontiguousarray(np.asarray(inputs["L0_attn_x"], dtype=np.float32))
    B_, Sq, C = x.shape
    xf = x.reshape(B_ * Sq, C)
    identb, identf, onehot, onesb, triL, triR = _host_constants()
    zeros128 = np.zeros((128, C), dtype=np.float32)
    zmask = np.zeros((128, 128), dtype=ml_dtypes.bfloat16)
    weights = {
        "Wqkv": np.ascontiguousarray(inputs["L0_attn_Wqkv"], dtype=np.float32),
        "bqkv": np.ascontiguousarray(inputs["L0_attn_bqkv"], dtype=np.float32),
        "Wout": np.ascontiguousarray(inputs["L0_attn_Wout"], dtype=np.float32),
        "bout": np.ascontiguousarray(inputs["L0_attn_bout"], dtype=np.float32),
        "router_w": np.ascontiguousarray(inputs["L0_ffn_router_w"], dtype=np.float32),
        "cbias": np.ascontiguousarray(inputs["L0_ffn_correction_bias"], dtype=np.float32),
        "cb128": np.ascontiguousarray(np.broadcast_to(
            np.asarray(inputs["L0_ffn_correction_bias"], dtype=np.float32)[None, :], (128, 16))),
        "gate_w": np.ascontiguousarray(inputs["L0_ffn_gate_w"], dtype=np.float32),
        "up_w": np.ascontiguousarray(inputs["L0_ffn_up_w"], dtype=np.float32),
        "down_w": np.ascontiguousarray(inputs["L0_ffn_down_w"], dtype=np.float32),
        "sgw": np.ascontiguousarray(inputs["L0_ffn_shared_gate_w"], dtype=np.float32),
        "suw": np.ascontiguousarray(inputs["L0_ffn_shared_up_w"], dtype=np.float32),
        "sdw": np.ascontiguousarray(inputs["L0_ffn_shared_down_w"], dtype=np.float32),
        "identb": identb, "identf": identf, "onehot": onehot, "onesb": onesb,
        "triL": triL, "triR": triR,
    }
    maps = []
    for c in range(8):
        seq = c // 2
        if c % 2 == 0:
            rows = np.concatenate([zeros128, xf[seq * 4096: seq * 4096 + 2176]], 0)
            mLe, mRe = zmask, triR
        else:
            rows = np.concatenate([xf[seq * 4096 + 1920: seq * 4096 + 4096], zeros128], 0)
            mLe, mRe = triL, zmask
        m = dict(weights)
        m["x_in"] = np.ascontiguousarray(rows)
        m["mLe"] = np.ascontiguousarray(mLe)
        m["mRe"] = np.ascontiguousarray(mRe)
        maps.append(m)
    return maps, (B_, Sq, C)


def kernel(**inputs):
    maps, (B_, Sq, C) = _in_maps(inputs)
    nc = build_nc("full")
    res = run_bass_kernel_spmd(nc, maps, core_ids=list(range(8)))
    out = np.concatenate([res.results[c]["y"] for c in range(8)], axis=0)
    return out.reshape(B_, Sq, C)


# version 12 (final)
# speedup vs baseline: 1.0357x; 1.0357x over previous
import contextlib
import numpy as np
import ml_dtypes
import concourse.bass as bass
import concourse.mybir as mybir
from concourse.bass_utils import run_bass_kernel_spmd

F32 = mybir.dt.float32
BF16 = mybir.dt.bfloat16
AF = mybir.ActivationFunctionType
ALU = mybir.AluOpType
AX = mybir.AxisListType

NT = 18
NTOK = NT * 128
OWN = list(range(1, 17))
D = 1024
KC = 8
NE = 16
NIT = 24
NH = 16
SLAB = 384

ENGS = ("sync", "tensor", "scalar", "vector", "gpsimd")


class Step:
    __slots__ = ("eng", "fn", "waits", "val", "pre", "block", "name", "sem")


class Sched:

    def __init__(self):
        self.steps = []
        self.ctr = {e: 0 for e in ENGS}
        self.block = 0
        self.semnames = list(ENGS)

    def dma_sem(self, name):
        if name not in self.ctr:
            self.ctr[name] = 0
            self.semnames.append(name)
        return name

    def add(self, eng, fn, waits=(), dma=0, name="", sem=None):
        s = Step()
        s.eng = eng
        s.fn = fn
        s.name = name
        s.block = self.block
        if dma:
            assert sem is not None, name
            s.sem = self.dma_sem(sem)
        else:
            assert sem is None
            s.sem = eng
        ws = {}
        for w in waits:
            if w is None:
                continue
            we, wv = w
            assert wv <= self.ctr[we], (name, we, wv, self.ctr[we])
            if wv > ws.get(we, -1):
                ws[we] = wv
        s.waits = ws
        s.pre = self.ctr[s.sem]
        self.ctr[s.sem] += 16 * dma if dma else 1
        s.val = self.ctr[s.sem]
        self.steps.append(s)
        return (s.sem, s.val)

    def total(self, sem):
        return (sem, self.ctr[sem])

    def fence(self, eng, waits, name="fence"):
        s = Step()
        s.eng = eng
        s.sem = eng
        s.fn = None
        s.name = name
        s.block = self.block
        ws = {}
        for w in waits:
            if w is None:
                continue
            we, wv = w
            if wv > ws.get(we, -1):
                ws[we] = wv
        s.waits = ws
        s.pre = s.val = self.ctr[eng]
        self.steps.append(s)
        return None

    def new_block(self):
        self.block += 1

    def emit(self, nc, sems):
        nblocks = self.block + 1
        for bi in range(nblocks):
            bsteps = [s for s in self.steps if s.block == bi]
            if not bsteps:
                continue
            with nc.Block() as blk:
                for eng in ENGS:
                    esteps = [s for s in bsteps if s.eng == eng]
                    if not esteps:
                        continue

                    def body(e, esteps=esteps, eng=eng):
                        last_w = {}
                        for s in esteps:
                            for we, wv in sorted(s.waits.items()):
                                if wv > last_w.get(we, -1):
                                    e.wait_ge(sems[we], wv)
                                    last_w[we] = wv
                            if s.fn is None:
                                continue
                            out = s.fn(e)
                            if isinstance(out, list):
                                for ins in out:
                                    ins.then_inc(sems[s.sem], 16)
                            else:
                                out.then_inc(sems[s.sem], 1)
                    getattr(blk, eng)(body)


def build_nc(upto="full"):
    nc = bass.Bass("TRN2", target_bir_lowering=False,
                   dynamic_dma_scratch_size=16384)

    T = {}
    def dram(name, shape, dt=F32, kind=None):
        if kind:
            T[name] = nc.dram_tensor(name, shape, dt, kind=kind)
        else:
            T[name] = nc.dram_tensor(name, shape, dt)
        return T[name]

    dram("x_in", [NTOK, D], kind="ExternalInput")
    dram("Wqkv", [D, 3 * D], kind="ExternalInput")
    dram("bqkv", [3 * D], kind="ExternalInput")
    dram("Wout", [D, D], kind="ExternalInput")
    dram("bout", [D], kind="ExternalInput")
    dram("router_w", [16, D], kind="ExternalInput")
    dram("cbias", [16], kind="ExternalInput")
    dram("cb128", [128, 16], kind="ExternalInput")
    dram("gate_w", [16, D, 256], kind="ExternalInput")
    dram("up_w", [16, D, 256], kind="ExternalInput")
    dram("down_w", [16, 256, D], kind="ExternalInput")
    dram("sgw", [2048, D], kind="ExternalInput")
    dram("suw", [2048, D], kind="ExternalInput")
    dram("sdw", [D, 2048], kind="ExternalInput")
    dram("identb", [128, 128], BF16, kind="ExternalInput")
    dram("identf", [128, 128], kind="ExternalInput")
    dram("onehot", [16, 2048], BF16, kind="ExternalInput")
    dram("onesb", [128, 128], BF16, kind="ExternalInput")
    dram("triL", [128, 128], BF16, kind="ExternalInput")
    dram("triR", [128, 128], BF16, kind="ExternalInput")
    dram("mLe", [128, 128], BF16, kind="ExternalInput")
    dram("mRe", [128, 128], BF16, kind="ExternalInput")

    out_rows = 2048 if upto == "full" else NTOK
    dram("y", [out_rows, D], kind="ExternalOutput")
    if upto == "dbgscr":
        dram("yscr", [NIT * D + 48 * 128, 512], BF16, kind="ExternalOutput")
    if upto == "dbgff":
        dram("yxn", [128, KC * NTOK], BF16, kind="ExternalOutput")
        dram("yf", [128, 288 * 6], kind="ExternalOutput")

    dram("s_geu", [NIT, 128, 4096], BF16)
    dram("s_down", [48 * 128, D], BF16)
    dram("s_wqkvQK", [16, 128, 8 * 128], BF16)
    dram("s_wqkvV", [D, D], BF16)
    dram("s_wout", [D, D], BF16)
    dram("q_scr", [8, 128, 2048], BF16)

    S = Sched()
    es = contextlib.ExitStack()
    with es:
        sbuf = lambda name, shape, dt=F32: es.enter_context(nc.sbuf_tensor(name, shape, dt))

        G = {}
        G["X"] = sbuf("X", [128, NT * D])
        for n, sh, dt in [("identb", [128, 128], BF16), ("identf", [128, 128], F32),
                          ("onehot", [16, 2048], BF16), ("onesb", [128, 128], BF16),
                          ("triL", [128, 128], BF16), ("triR", [128, 128], BF16),
                          ("mLe", [128, 128], BF16), ("mRe", [128, 128], BF16)]:
            G[n] = sbuf("c_" + n, sh, dt)
        G["rwT"] = sbuf("c_rwT", [128, 8 * 16], BF16)
        G["cb"] = sbuf("c_cb", [128, 16])
        G["bqk"] = sbuf("c_bqk", [128, 16])
        G["bvb"] = sbuf("c_bvb", [1, D], BF16)
        G["bob"] = sbuf("c_bob", [1, D], BF16)
        _c2 = sbuf("c_eps2", [128, 2])
        G["eps"] = _c2[:, 0:1]
        G["s32"] = _c2[:, 1:2]
        for n in ("sum", "sq", "mean", "rstd", "nmr"):
            G["st_" + n] = sbuf("st_" + n, [128, NT])


        _prologue_early(nc, S, T, G, es)
        with contextlib.ExitStack() as es_cast:
            shared_ready = _cast_shared(nc, S, T, G, es_cast)
            pool_ready = _cast_pool(nc, S, T, G, es_cast)
            ready = dict(pool_ready)
            for kk in ("shared", "sd", "geu", "down"):
                ready[kk] = shared_ready[kk]
            G["_ready"] = ready

            import os as _os
            if _os.environ.get("KBAR"):
                S.new_block()
            if upto == "dbgscr":
                S.new_block()
                toks = []
                for b in range(NIT):
                    def f(e, b=b):
                        return [e.dma_start(out=T["yscr"][b * D:(b + 1) * D, :],
                                            in_=T["s_geu"][b, :, :])]
                    toks.append(S.add("sync", f, dma=1, sem="out", name="dscr"))
                def f2(e):
                    return [e.dma_start(out=T["yscr"][NIT * D:NIT * D + 48 * 128, 0:512],
                                        in_=T["s_down"][:, 0:512])]
                toks.append(S.add("sync", f2, dma=1, sem="out", name="dscr2"))
                S.fence("sync", toks, name="dscr_f")
            if upto.startswith("pro"):
                f1_done = S.total("ldx")
            else:
                f1_done = _ffn(nc, S, T, G, es, tiles=list(range(NT)), tag="f1",
                               prev=S.total("ldx"))
        S.new_block()

        if upto == "dbgff":
            def fd(e):
                return [e.dma_start(out=T["yxn"][:, :], in_=G["_dbg_xn"][:, :]),
                        e.dma_start(out=T["yf"][:, 0:288], in_=G["_dbg_sc"][:, :]),
                        e.dma_start(out=T["yf"][:, 288:576], in_=G["_dbg_ca"][:, :]),
                        e.dma_start(out=T["yf"][:, 576:594], in_=G["st_mean"][:, :]),
                        e.dma_start(out=T["yf"][:, 594:612], in_=G["st_rstd"][:, :]),
                        ]
            t = S.add("sync", fd, waits=[f1_done], dma=1, sem="out", name="dbgff")
            S.fence("sync", [t], name="dbgff_f")
        att_done = f1_done
        if upto in ("full", "attn"):
            att_done = _attention(nc, S, T, G, es, prev=f1_done)
            S.new_block()

        f2_done = att_done
        if upto == "full":
            f2_done = _ffn(nc, S, T, G, es, tiles=OWN, tag="f2", prev=att_done)
            S.new_block()

        def emit_out(t, i):
            def f(e):
                return [e.dma_start(out=T["y"][i * 128:(i + 1) * 128, :],
                                    in_=G["X"][:, t * D:(t + 1) * D])]
            return f
        tiles = OWN if upto == "full" else list(range(NT))
        outs = []
        for i, t in enumerate(tiles):
            outs.append(S.add("sync", emit_out(t, i), waits=[f2_done], dma=1, sem="out", name="out"))
        S.fence("sync", outs, name="out_fence")

        sems = {n: es.enter_context(nc.semaphore("sem_" + n)) for n in S.semnames}
        S.emit(nc, sems)
    return nc


def _prologue_early(nc, S, T, G, es):
    X = G["X"]
    for t in range(NT):
        def f(e, t=t):
            return [e.dma_start(out=X[:, t * D:(t + 1) * D],
                                in_=T["x_in"][t * 128:(t + 1) * 128, :])]
        S.add("sync", f, dma=1, sem="ldx", name="ldX")
    for n in ("identb", "identf", "onehot", "onesb", "triL", "triR", "mLe", "mRe"):
        def f(e, n=n):
            return [e.dma_start(out=G[n][:, :], in_=T[n][:, :])]
        S.add("sync", f, dma=1, sem="ldc", name="ldC")

    with contextlib.ExitStack() as es2:
        sb2 = lambda name, shape, dt=F32: es2.enter_context(nc.sbuf_tensor(name, shape, dt))
        t24 = sb2("p_t24", [24, 128])
        t16 = sb2("p_t16", [16, 1024])
        t16b = sb2("p_t16b", [16, 1024], BF16)
        trow = sb2("p_trow", [1, 2048])
        pps = es2.enter_context(nc.psum_tensor("p_pps", [128, 512], F32))
        ppb = es2.enter_context(nc.psum_tensor("p_ppb", [128, 512], BF16))

        l1 = S.add("sync", lambda e: [
            e.dma_start(out=t24[0:16, :],
                        in_=T["bqkv"].rearrange("(c p) -> c p", p=128)[0:16, :]),
            e.dma_start(out=trow[0:1, 0:1024], in_=T["bqkv"][None, 2048:3072]),
            e.dma_start(out=trow[0:1, 1024:2048], in_=T["bout"][None, :]),
            e.dma_start(out=G["cb"][:, :], in_=T["cb128"][:, :]),
            e.dma_start(out=t16[:, :], in_=T["router_w"][:, :]),
        ], dma=5, sem="lds", name="ld_small")

        v1 = S.add("vector", lambda e: e.memset(G["eps"], 1e-5))
        S.add("vector", lambda e: e.memset(G["s32"], 1.0 / 32.0), waits=[v1])
        v2 = S.add("vector", lambda e: e.tensor_copy(out=G["bvb"][:, :], in_=trow[0:1, 0:1024]),
                   waits=[l1])
        v3 = S.add("vector", lambda e: e.tensor_copy(out=G["bob"][:, :], in_=trow[0:1, 1024:2048]),
                   waits=[l1])
        v4 = S.add("vector", lambda e: e.tensor_copy(out=t16b[:, :], in_=t16[:, :]), waits=[l1])
        p1 = S.add("tensor", lambda e: e.matmul(pps[:, 0:16], t24[0:16, :],
                                                G["identf"][0:16, 0:16],
                                                is_transpose=True, start=True, stop=True),
                   waits=[l1, S.total("ldc")], name="tr_bqk")
        def rw_tr(e):
            for k in range(KC):
                ins = e.matmul(ppb[:, k * 16:(k + 1) * 16], t16b[0:16, k * 128:(k + 1) * 128],
                               G["identb"][0:16, 0:16], is_transpose=True, start=True, stop=True)
            return ins
        p2 = S.add("tensor", rw_tr, waits=[v4, S.total("ldc")], name="tr_rw")
        bqk_ev = S.add("vector", lambda e: e.tensor_copy(out=G["bqk"][:, :], in_=pps[:, 0:16]),
                       waits=[p1])
        rw_ev = S.add("vector", lambda e: e.tensor_copy(out=G["rwT"][:, :], in_=ppb[:, 0:128]),
                      waits=[p2])
    G["_rw_ev"] = rw_ev
    G["_bqk_ev"] = bqk_ev


def _cast_shared(nc, S, T, G, es2):
    ready = {"shared": {}, "sd": {}}
    with contextlib.ExitStack() as es_ps:
        sb2 = lambda name, shape, dt=F32: es2.enter_context(nc.sbuf_tensor(name, shape, dt))
        fb2 = sb2("cs_fb", [128, 2 * 512])
        bb2 = sb2("cs_bb", [128, 2 * 512], BF16)
        sTd = sb2("cs_sTd", [128, 2 * 1024], BF16)
        pta = es_ps.enter_context(nc.psum_tensor("cs_pta", [128, 128], BF16))
        ptb = es_ps.enter_context(nc.psum_tensor("cs_ptb", [128, 128], BF16))
        pbank = [pta, ptb]

        sg_st = {}
        sd_st = {}
        cast_tok = {}
        tr_last = {}
        ev_hist = []
        st_slot = {0: None, 1: None}
        tr_idx = [0]
        ji = [0]
        ds_ctr = [0]
        ds_last = {}

        def _ds_store(fn, waits, ndma=1, name="cs_st"):
            k = ds_ctr[0]
            ds_ctr[0] += 1
            p = k % 2
            w = list(waits) + ([ds_last[p]] if p in ds_last else [])
            tok = S.add("scalar", fn, waits=w, dma=ndma,
                        sem="dsA" if p == 0 else "dsB", name=name)
            ds_last[p] = tok
            return tok

        def job(ld_fn, ntr, tr_in_fn, ev_out_fn, extra_w=(), st_fn=None):
            i = ji[0]
            ji[0] += 1
            off = (i % 2) * 512
            wt = [cast_tok[i - 2]] if (i - 2) in cast_tok else [G["_rw_ev"], G["_bqk_ev"]]
            dw = S.add("sync", lambda e: [ld_fn(e, fb2, off)],
                       waits=wt + list(extra_w), dma=1,
                       sem="fbA" if i % 2 == 0 else "fbB", name="cs_ld")
            cw = [dw]
            if i >= 2:
                cw.append(tr_last[i - 2])
            cv = S.add("vector", lambda e: e.tensor_copy(out=bb2[:, off:off + 512],
                                                         in_=fb2[:, off:off + 512]),
                       waits=cw, name="cs_cast")
            cast_tok[i] = cv
            toks = []
            if st_fn is not None:
                stok = _ds_store(lambda e: st_fn(e, bb2, off), [cv], name="cs_dst")
                tr_last[i] = stok
                return i, [stok]
            for u in range(ntr):
                ti = tr_idx[0]
                tr_idx[0] += 1
                w2 = [cv]
                if len(ev_hist) >= 2:
                    w2.append(ev_hist[-2])
                pt = pbank[ti % 2]
                ptk = S.add("tensor", lambda e, u=u, off=off, pt=pt:
                            e.matmul(pt[:, :], tr_in_fn(bb2, off, u), G["identb"][:, :],
                                     is_transpose=True, start=True, stop=True),
                            waits=w2, name="cs_tr")
                ev = S.add("scalar", lambda e, u=u, pt=pt: ev_out_fn(e, pt, u),
                           waits=[ptk], name="cs_ev")
                ev_hist.append(ev)
                toks.append(ev)
            tr_last[i] = toks[-1] if toks else cv
            return i, toks

        def store(st_fn, waits, ndma=1):
            return _ds_store(lambda e: st_fn(e), waits, ndma=ndma)

        def emit_geu(e_lo, e_hi):
            for e_i in range(e_lo, e_hi):
                for (wsrc, c0) in ((T["gate_w"], 0), (T["up_w"], 256)):
                    for kh in range(4):
                        def ld(e, fbuf, off, wsrc=wsrc, e_i=e_i, kh=kh):
                            src = wsrc[e_i, :, :].rearrange("(k p) c -> p k c", p=128)[
                                :, kh * 2:(kh + 1) * 2, :]
                            return e.dma_start(out=fbuf[:, off:off + 512]
                                               .rearrange("p (k c) -> p k c", c=256),
                                               in_=src)
                        def st(e, bbuf, off, e_i=e_i, c0=c0, kh=kh):
                            dst = T["s_geu"][e_i, :, :].rearrange(
                                "p (k c) -> p k c", c=512)[:, kh * 2:kh * 2 + 2, c0:c0 + 256]
                            return [e.dma_start(out=dst,
                                                in_=bbuf[:, off:off + 512]
                                                .rearrange("p (k c) -> p k c", c=256))]
                        _, tk = job(ld, 0, None, None, st_fn=st)
                ready["geu"][e_i] = tk[0]

        def emit_down():
            for e_i in range(NE):
                for hh in range(2):
                    for ch in range(2):
                        def ld(e, fbuf, off, e_i=e_i, hh=hh, ch=ch):
                            return e.dma_start(out=fbuf[:, off:off + 512],
                                               in_=T["down_w"][e_i, hh * 128:(hh + 1) * 128,
                                                               ch * 512:(ch + 1) * 512])
                        def st(e, bbuf, off, e_i=e_i, hh=hh, ch=ch):
                            return [e.dma_start(
                                out=T["s_down"][(2 * e_i + hh) * 128:(2 * e_i + hh + 1) * 128,
                                                ch * 512:(ch + 1) * 512],
                                in_=bbuf[:, off:off + 512])]
                        _, tk = job(ld, 0, None, None, st_fn=st)
                    ready["down"][2 * e_i + hh] = tk[0]

        ready["geu"] = {}
        ready["down"] = {}
        emit_geu(0, 4)
        for (wsrc, c0) in ((T["sgw"], 0), (T["suw"], 256)):
            for j in range(16):
                pe_i = j // 2
                toks2 = []
                for hf in range(2):
                    def ld(e, fbuf, off, wsrc=wsrc, j=j, hf=hf):
                        return e.dma_start(out=fbuf[:, off:off + 512],
                                           in_=wsrc[j * 128:(j + 1) * 128,
                                                    hf * 512:(hf + 1) * 512])
                    def trin(bbuf, off, k):
                        return bbuf[:, off + k * 128:off + (k + 1) * 128]
                    def evout(e, pt, k, j=j, hf=hf):
                        kk = hf * 4 + k
                        return e.copy(out=sTd[:, (j % 2) * 1024 + kk * 128:
                                              (j % 2) * 1024 + (kk + 1) * 128],
                                      in_=pt[:, :])
                    ew = [sg_st[j - 2]] if (j - 2) in sg_st else []
                    _, toks = job(ld, 4, trin, evout, extra_w=ew)
                    toks2.append(toks[-1])
                def stf(e, c0=c0, j=j, pe_i=pe_i):
                    base = c0 + (j % 2) * 128
                    dst = T["s_geu"][16 + pe_i, :, :].rearrange(
                        "p (k c) -> p k c", c=512)[:, :, base:base + 128]
                    src = sTd[:, (j % 2) * 1024:(j % 2) * 1024 + 1024].rearrange(
                        "p (k c) -> p k c", c=128)
                    return [e.dma_start(out=dst, in_=src)]
                tok = store(stf, [toks2[-1]], ndma=1)
                sg_st[j] = tok
                if j % 2 == 1:
                    ready["shared"][pe_i] = tok
            sg_st[-2], sg_st[-1] = sg_st[14], sg_st[15]
            for j in list(sg_st):
                if j >= 0:
                    del sg_st[j]
        emit_geu(4, NE)
        emit_down()
        for h in range(16):
            toks2 = []
            for hf in range(2):
                def ld(e, fbuf, off, h=h, hf=hf):
                    src = T["sdw"].rearrange("(q p) (h c) -> p h q c", p=128, c=128)[
                        :, h, hf * 4:(hf + 1) * 4, :]
                    return e.dma_start(out=fbuf[:, off:off + 512]
                                       .rearrange("p (q c) -> p q c", c=128), in_=src)
                def trin(bbuf, off, k):
                    return bbuf[:, off + k * 128:off + (k + 1) * 128]
                def evout(e, pt, k, h=h, hf=hf):
                    kk = hf * 4 + k
                    return e.copy(out=sTd[:, (h % 2) * 1024 + kk * 128:
                                          (h % 2) * 1024 + (kk + 1) * 128],
                                  in_=pt[:, :])
                ew = [sd_st[h - 2]] if (h - 2) in sd_st else [sg_st[-2 + (h % 2)]]
                _, toks = job(ld, 4, trin, evout, extra_w=ew)
                toks2.append(toks[-1])
            def stf(e, h=h):
                return [e.dma_start(out=T["s_down"][4096 + h * 128:4096 + (h + 1) * 128, :],
                                    in_=sTd[:, (h % 2) * 1024:(h % 2) * 1024 + 1024])]
            tok = store(stf, [toks2[-1]], ndma=1)
            sd_st[h] = tok
            ready["sd"][h] = tok
    return ready


def _cast_pool(nc, S, T, G, es):
    ready = {"geu": {}, "down": {}}
    fb3 = es.enter_context(nc.sbuf_tensor("cp_fb", [128, 2 * 256], F32))
    bb3 = es.enter_context(nc.sbuf_tensor("cp_bb", [128, 2 * 256], BF16))
    st_slot = {0: None, 1: None}
    ji = [0]

    def job2(ld_fn, st_fn, name):
        i = ji[0]
        ji[0] += 1
        off = (i % 2) * 256
        wt = [st_slot[i % 2]] if st_slot[i % 2] else [G["_rw_ev"], G["_bqk_ev"]]
        dw = S.add("gpsimd", lambda e: [ld_fn(e, fb3, off)], waits=wt, dma=1,
                   sem="gpdA" if i % 2 == 0 else "gpdB", name=name + "_ld")
        cv = S.add("gpsimd", lambda e: e.tensor_copy(out=bb3[:, off:off + 256],
                                                     in_=fb3[:, off:off + 256]),
                   waits=[dw], name=name + "_cast")
        st = S.add("gpsimd", lambda e: st_fn(e, bb3, off), waits=[cv], dma=1,
                   sem="gpsA" if i % 2 == 0 else "gpsB", name=name + "_st")
        st_slot[i % 2] = st
        return st

    def job(ld_fn, st_fn, name):
        def mk(ch, q01):
            def ld2(e, fbuf, off, ch=ch, q01=q01):
                return ld_fn(e, fbuf, off, ch, q01)
            def st2(e, bbuf, off, ch=ch, q01=q01):
                return st_fn(e, bbuf, off, ch, q01)
            return ld2, st2
        for ch in range(2):
            for q01 in range(2):
                l2, s2 = mk(ch, q01)
                tok = job2(l2, s2, name)
        return tok

    for k in range(KC):
        for part in range(3):
            def ld(e, fbuf, off, ch, q01, k=k, part=part):
                cq = ch * 2 + q01
                return e.dma_start(out=fbuf[:, off:off + 256],
                                   in_=T["Wqkv"][k * 128:(k + 1) * 128,
                                                 part * 1024 + cq * 256:part * 1024 + (cq + 1) * 256])
            if part < 2:
                def st(e, bbuf, off, ch, q01, k=k, part=part):
                    cq = ch * 2 + q01
                    dst = T["s_wqkvQK"].rearrange("oc p c -> p oc c")[
                        :, part * 8 + cq * 2:part * 8 + (cq + 1) * 2, k * 128:(k + 1) * 128]
                    return [e.dma_start(out=dst, in_=bbuf[:, off:off + 256]
                                        .rearrange("p (oc c) -> p oc c", c=128))]
            else:
                def st(e, bbuf, off, ch, q01, k=k):
                    cq = ch * 2 + q01
                    return [e.dma_start(out=T["s_wqkvV"][k * 128:(k + 1) * 128,
                                                         cq * 256:(cq + 1) * 256],
                                        in_=bbuf[:, off:off + 256])]
            tok = job(ld, st, "qkv")
    ready["qkv"] = tok
    for k in range(KC):
        def ld(e, fbuf, off, ch, q01, k=k):
            cq = ch * 2 + q01
            return e.dma_start(out=fbuf[:, off:off + 256],
                               in_=T["Wout"][k * 128:(k + 1) * 128, cq * 256:(cq + 1) * 256])
        def st(e, bbuf, off, ch, q01, k=k):
            cq = ch * 2 + q01
            return [e.dma_start(out=T["s_wout"][k * 128:(k + 1) * 128, cq * 256:(cq + 1) * 256],
                                in_=bbuf[:, off:off + 256])]
        tok = job(ld, st, "wo")
    ready["wout"] = tok
    return ready


def _ln_stats(nc, S, G, tiles, prev, scr, coff=0):
    n = len(tiles)
    red = []
    sq = []
    for i, t in enumerate(tiles):
        ii = coff + i
        red.append(S.add("vector", lambda e, ii=ii, t=t:
                         e.reduce_sum(out=G["st_sum"][:, ii:ii + 1],
                                      in_=G["X"][:, t * D:(t + 1) * D], axis=AX.X),
                         waits=[prev], name="ln_red"))
        wsq = [prev]
        if i >= 1:
            wsq.append(sq[i - 1])
        elif G.get("_last_sq"):
            wsq.append(G["_last_sq"])
        sq.append(S.add("scalar", lambda e, ii=ii, t=t:
                        e.activation(out=scr[:, 0:D],
                                     in_=G["X"][:, t * D:(t + 1) * D],
                                     func=AF.Square, accum_out=G["st_sq"][:, ii:ii + 1]),
                        waits=wsq, name="ln_sq"))
    G["_last_sq"] = sq[-1]
    a, b = coff, coff + n
    w = [red[-1], sq[-1]]
    s1 = S.add("vector", lambda e: e.tensor_scalar_mul(G["st_mean"][:, a:b],
                                                       G["st_sum"][:, a:b], 1.0 / D),
               waits=w, name="ln_mean")
    s2 = S.add("vector", lambda e: e.tensor_mul(out=G["st_nmr"][:, a:b],
                                                in0=G["st_mean"][:, a:b],
                                                in1=G["st_mean"][:, a:b]),
               waits=[s1], name="ln_m2")
    s3 = S.add("vector", lambda e: e.tensor_scalar_mul(G["st_sq"][:, a:b],
                                                       G["st_sq"][:, a:b], 1.0 / D),
               waits=[s2], name="ln_msq")
    s4 = S.add("vector", lambda e: e.tensor_sub(out=G["st_sq"][:, a:b],
                                                in0=G["st_sq"][:, a:b],
                                                in1=G["st_nmr"][:, a:b]),
               waits=[s3], name="ln_var")
    s5 = S.add("scalar", lambda e: e.activation(out=G["st_rstd"][:, a:b],
                                                in_=G["st_sq"][:, a:b],
                                                func=AF.Sqrt, bias=G["eps"]),
               waits=[s4], name="ln_sqrt")
    s6 = S.add("vector", lambda e: e.reciprocal(out=G["st_rstd"][:, a:b],
                                                in_=G["st_rstd"][:, a:b]),
               waits=[s5], name="ln_recip")
    return s6


def _ln_apply(nc, S, G, tiles, stats_tok, xn_fm, xn_tm, W, prev, nslots=2,
              coff=0, sbase=0, prev_tr=(), pe_pt=None):
    tr_tok = {}
    app_tok = {}
    ev_hist = []
    for i, t in enumerate(tiles):
        w = [stats_tok, prev]
        if i >= nslots:
            w.append(tr_tok[i - nslots])
        elif prev_tr:
            w.append(prev_tr[i % len(prev_tr)])
        sl = (sbase + i % nslots) * D
        ii = coff + i
        app_tok[i] = S.add("vector", lambda e, ii=ii, t=t, sl=sl:
                           e.tensor_scalar(out=xn_tm[:, sl:sl + D],
                                           in0=G["X"][:, t * D:(t + 1) * D],
                                           scalar1=G["st_mean"][:, ii:ii + 1],
                                           scalar2=G["st_rstd"][:, ii:ii + 1],
                                           op0=ALU.subtract, op1=ALU.mult),
                           waits=w, name="ln_app")
        if pe_pt is not None:
            for u in range(KC):
                ti = i * KC + u
                pt = pe_pt[ti % 2]
                w2 = [app_tok[i]]
                if len(ev_hist) >= 2:
                    w2.append(ev_hist[-2])
                ptk = S.add("tensor", lambda e, sl=sl, u=u, pt=pt:
                            e.matmul(pt[:, :], xn_tm[:, sl + u * 128:sl + (u + 1) * 128],
                                     G["identb"][:, :], is_transpose=True,
                                     start=True, stop=True),
                            waits=w2, name="xn_tr")
                ev = S.add("scalar", lambda e, i=i, u=u, pt=pt:
                           e.copy(out=xn_fm[:, u * W + i * 128:u * W + (i + 1) * 128],
                                  in_=pt[:, :]),
                           waits=[ptk], name="xn_ev")
                ev_hist.append(ev)
            tr_tok[i] = ev_hist[-1]
            continue
        def trf(e, i=i, sl=sl):
            outs = []
            for k in range(KC):
                outs.append(e.dma_start(
                    out=xn_fm[:, k * W + i * 128:k * W + (i + 1) * 128],
                    in_=xn_tm[:, sl + k * 128:sl + (k + 1) * 128],
                    transpose=True))
            return outs
        tr_tok[i] = S.add("scalar" if i % 2 == 0 else "sync", trf,
                          waits=[app_tok[i]], dma=KC,
                          sem="trA" if i % 2 == 0 else "trB", name="ln_tr")
    return tr_tok


def _ffn(nc, S, T, G, es, tiles, tag, prev=None):
    ntile = len(tiles)
    W = ntile * 128
    nslab = (W + SLAB - 1) // SLAB

    es2 = contextlib.ExitStack()
    with es2:
        sb2 = lambda name, shape, dt=F32: es2.enter_context(nc.sbuf_tensor(tag + name, shape, dt))
        xn_fm = sb2("xn", [128, KC * W], BF16)
        xn_tm = sb2("xt", [128, D], BF16)
        h_buf = sb2("h", [128, 48 * SLAB], BF16)
        wbuf = sb2("w", [128, 2 * 4096], BF16)
        dbuf = sb2("d", [128, 4 * D], BF16)
        s_scr = sb2("s", [128, 2 * SLAB])
        scores = sb2("sc", [128, ntile * 16])
        r_sb = sb2("rs", [128, ntile * 16])
        r_g2 = sb2("g2", [128, ntile * 4])
        r_gs = sb2("gs", [128, ntile * 4])
        r_cnt = sb2("cn", [128, ntile * 4])
        r_msk = sb2("mk", [128, ntile * 16])
        r_msb = sb2("mb", [128, ntile * 16])
        r_m = G["st_nmr"][:, 0:ntile]
        r_sel = sb2("se", [128, ntile * 16])
        r_w = r_sb
        r_dn = r_g2[:, 0:ntile]
        c_all = r_msk
        G["_dbg_xn"] = xn_fm
        G["_dbg_sc"] = scores
        G["_dbg_ca"] = c_all
        ckm_b = sb2("ck", [16, W], BF16)

        pD0 = es2.enter_context(nc.psum_tensor(tag + "pD0", [128, D], F32))
        pD1 = es2.enter_context(nc.psum_tensor(tag + "pD1", [128, D], F32))
        es_pt = contextlib.ExitStack()
        pe_pt = (es_pt.enter_context(nc.psum_tensor(tag + "ptA", [128, 128], BF16)),
                 es_pt.enter_context(nc.psum_tensor(tag + "ptB", [128, 128], BF16)))

        stats = _ln_stats(nc, S, G, tiles, prev, xn_tm)
        tr_tok = _ln_apply(nc, S, G, tiles, stats, xn_fm, xn_tm, W, prev, nslots=1,
                           pe_pt=pe_pt)

        sig_tok = {}
        log_tok = {}
        for i, t in enumerate(tiles):
            w = [tr_tok[i], G["_rw_ev"]]
            if i >= 1:
                w.append(sig_tok[i - 1])
            def lf(e, i=i):
                for k in range(KC):
                    ins = e.matmul(pD0[:, 0:16],
                                   xn_fm[:, k * W + i * 128:k * W + (i + 1) * 128],
                                   G["rwT"][:, k * 16:(k + 1) * 16],
                                   start=(k == 0), stop=(k == KC - 1))
                return ins
            log_tok[i] = S.add("tensor", lf, waits=w, name="r_log")
            sig_tok[i] = S.add("scalar", lambda e, i=i:
                               e.activation(out=scores[:, i * 16:(i + 1) * 16],
                                            in_=pD0[:, 0:16], func=AF.Sigmoid),
                               waits=[log_tok[i]], name="r_sig")

        c_tok = _routing(nc, S, G, ntile, scores, r_sb, r_g2, r_gs, r_cnt, r_msk,
                         r_msb, r_m, r_sel, r_w, r_dn, c_all,
                         sig_tok[ntile - 1])

        ck_ev = {}
        ldc_tot = S.total("ldc")
        for i in range(ntile):
            w = [c_tok, ldc_tot]
            if i >= 2:
                w.append(ck_ev[i - 2])
            ptk = S.add("tensor", lambda e, i=i:
                        e.matmul(pD1[0:16, (i % 2) * 512:(i % 2) * 512 + 128],
                                 c_all[:, i * 16:(i + 1) * 16],
                                 G["identf"][:, :], is_transpose=True, start=True, stop=True),
                        waits=w, name="c_tr")
            ck_ev[i] = S.add("scalar", lambda e, i=i:
                             e.copy(out=ckm_b[0:16, i * 128:(i + 1) * 128],
                                    in_=pD1[0:16, (i % 2) * 512:(i % 2) * 512 + 128]),
                             waits=[ptk], name="c_ev")
        ck_done = ck_ev[ntile - 1]
        es_pt.close()

        pG0 = es2.enter_context(nc.psum_tensor(tag + "pG0", [128, SLAB], F32))
        pU0 = es2.enter_context(nc.psum_tensor(tag + "pU0", [128, SLAB], F32))
        pG1 = es2.enter_context(nc.psum_tensor(tag + "pG1", [128, SLAB], F32))
        pU1 = es2.enter_context(nc.psum_tensor(tag + "pU1", [128, SLAB], F32))

        phaseB_last = (ck_done, log_tok[ntile - 1], None, None)
        for s in range(nslab):
            tok0 = s * SLAB
            stoks = min(SLAB, W - tok0)
            stc = stoks // 128
            phaseB_last = _ffn_slab(nc, S, T, G, tiles, s, tok0, stoks, stc, W,
                                    xn_fm, h_buf, wbuf, dbuf, s_scr, ckm_b,
                                    pG0, pU0, pG1, pU1, pD0, pD1,
                                    ck_done, tr_tok, phaseB_last)
        return phaseB_last[0]


def _routing(nc, S, G, ntile, scores, r_sb, r_g2, r_gs, r_cnt, r_msk, r_msb,
             r_m, r_sel, r_w, r_dn, c_all, prev):
    n16 = ntile * 16
    n4 = ntile * 4
    tok = prev
    lds_w = [S.total("lds")]

    def V(fn, name):
        nonlocal tok
        tok = S.add("vector", fn, waits=[tok], name=name)
        return tok

    for t in range(ntile):
        tok = S.add("vector", lambda e, t=t: e.tensor_add(out=r_sb[:, t * 16:(t + 1) * 16],
                                                          in0=scores[:, t * 16:(t + 1) * 16],
                                                          in1=G["cb"][:, :]),
                    waits=[tok] + lds_w, name="rt_sb")
    sbg = r_sb[:, 0:n16].rearrange("p (tg e) -> p tg e", e=4)
    g2v = r_g2[:, 0:n4]
    gsv = r_gs[:, 0:n4]
    pairs = [(0, 1), (0, 2), (0, 3), (1, 2), (1, 3), (2, 3)]
    for pi, (i, j) in enumerate(pairs):
        V(lambda e, i=i, j=j: e.tensor_add(out=g2v, in0=sbg[:, :, i], in1=sbg[:, :, j]), "rt_pair")
        if pi == 0:
            V(lambda e: e.tensor_copy(out=gsv, in_=g2v), "rt_cp")
        else:
            V(lambda e: e.tensor_max(out=gsv, in0=gsv, in1=g2v), "rt_max")
    V(lambda e: e.memset(r_cnt[:, 0:n4], 0.0), "rt_z")
    gst = r_gs[:, 0:n4].rearrange("p (t g) -> p t g", g=4)
    g2t = r_g2[:, 0:n4].rearrange("p (t g) -> p t g", g=4)
    for sft in range(1, 4):
        m = 4 - sft
        V(lambda e, m=m, sft=sft: e.tensor_tensor(out=g2t[:, :, 0:m], in0=gst[:, :, 0:m],
                                                  in1=gst[:, :, sft:4], op=ALU.is_lt), "rt_c1")
        V(lambda e, m=m, sft=sft: e.tensor_tensor(out=g2t[:, :, m:4], in0=gst[:, :, m:4],
                                                  in1=gst[:, :, 0:sft], op=ALU.is_lt), "rt_c2")
        V(lambda e: e.tensor_add(out=r_cnt[:, 0:n4], in0=r_cnt[:, 0:n4], in1=g2v), "rt_acc")
    V(lambda e: e.tensor_single_scalar(out=r_cnt[:, 0:n4], in_=r_cnt[:, 0:n4],
                                       scalar=2.0, op=ALU.is_lt), "rt_gm")
    mskv = r_msk[:, 0:n16].rearrange("p (tg e) -> p tg e", e=4)
    for e4 in range(4):
        V(lambda e, e4=e4: e.tensor_copy(out=mskv[:, :, e4], in_=r_cnt[:, 0:n4]), "rt_exp")
    V(lambda e: e.tensor_scalar(out=r_msb[:, 0:n16], in0=r_msk[:, 0:n16],
                                scalar1=1e9, scalar2=-1e9, op0=ALU.mult, op1=ALU.add), "rt_pen")
    V(lambda e: e.tensor_mul(out=r_w[:, 0:n16], in0=r_sb[:, 0:n16], in1=r_msk[:, 0:n16]), "rt_mul")
    V(lambda e: e.tensor_add(out=r_msb[:, 0:n16], in0=r_msb[:, 0:n16], in1=r_w[:, 0:n16]), "rt_msb")
    V(lambda e: e.memset(r_sel[:, 0:n16], 0.0), "rt_zs")
    msbv = r_msb[:, 0:n16].rearrange("p (t e) -> p t e", e=16)
    for it in range(4):
        V(lambda e: e.tensor_reduce(out=r_m[:, 0:ntile], in_=msbv, axis=AX.X, op=ALU.max),
          "rt_rm")
        for t in range(ntile):
            V(lambda e, t=t: e.tensor_scalar(out=r_w[:, t * 16:(t + 1) * 16],
                                             in0=r_msb[:, t * 16:(t + 1) * 16],
                                             scalar1=r_m[:, t:t + 1], scalar2=None,
                                             op0=ALU.is_ge), "rt_ge")
        V(lambda e: e.tensor_add(out=r_sel[:, 0:n16], in0=r_sel[:, 0:n16],
                                 in1=r_w[:, 0:n16]), "rt_sel")
        if it < 3:
            V(lambda e: e.scalar_tensor_tensor(out=r_msb[:, 0:n16], in0=r_w[:, 0:n16],
                                               scalar=-2e9, in1=r_msb[:, 0:n16],
                                               op0=ALU.mult, op1=ALU.add), "rt_rem")
    V(lambda e: e.tensor_mul(out=r_w[:, 0:n16], in0=scores[:, 0:n16],
                             in1=r_sel[:, 0:n16]), "rt_w")
    V(lambda e: e.tensor_reduce(out=r_dn[:, 0:ntile],
                                in_=r_w[:, 0:n16].rearrange("p (t e) -> p t e", e=16),
                                axis=AX.X, op=ALU.add), "rt_dn")
    V(lambda e: e.tensor_scalar_add(r_dn[:, 0:ntile], r_dn[:, 0:ntile], 1e-20), "rt_eps")
    V(lambda e: e.reciprocal(out=r_dn[:, 0:ntile], in_=r_dn[:, 0:ntile]), "rt_rc")
    for t in range(ntile):
        V(lambda e, t=t: e.tensor_scalar(out=c_all[:, t * 16:(t + 1) * 16],
                                         in0=r_w[:, t * 16:(t + 1) * 16],
                                         scalar1=r_dn[:, t:t + 1], scalar2=None,
                                         op0=ALU.mult), "rt_c")
    return tok


def _ffn_slab(nc, S, T, G, tiles, s, tok0, stoks, stc, W,
              xn_fm, h_buf, wbuf, dbuf, s_scr, ckm_b,
              pG0, pU0, pG1, pU1, pD0, pD1, ck_done, tr_tok, phaseB_prev):
    X = G["X"]
    RD = G["_ready"]
    pBtok, pB_pD_free, wb_free, db_free = phaseB_prev

    dw_w = {}
    pe_gu = {}
    pe_gu_g = {}
    pe_c = {}
    ac_silu = {}
    dv_h = {}
    RD = G["_ready"]

    def emit_wld(it, w):
        w = list(w) + [RD["geu"][it] if it < NE else RD["shared"][it - NE]]
        dw_w[it] = S.add("sync", lambda e, it=it:
                         [e.dma_start(out=wbuf[:, (it % 2) * 4096:(it % 2 + 1) * 4096],
                                      in_=T["s_geu"][it, :, :])],
                         waits=w, dma=1, sem="wbA" if it % 2 == 0 else "wbB",
                         name="gu_ld")
    emit_wld(0, [wb_free or pBtok])
    emit_wld(1, [wb_free or pBtok])
    for it in range(NIT):
        routed = it < NE
        wb = (it % 2) * 4096
        pC = pD0 if it % 2 == 0 else pD1
        pCs = lambda p: p[:, 0:stoks]
        if routed:
            w = [S.total("ldc"), ck_done]
            if it >= 2:
                w.append(dv_h.get((it - 2, 1)))
            else:
                w.append(pBtok)
                w.append(pB_pD_free)
            pe_c[it] = S.add("tensor", lambda e, it=it, pC=pC:
                             e.matmul(pC[:, 0:stoks],
                                      G["onehot"][0:16, it * 128:(it + 1) * 128],
                                      ckm_b[0:16, tok0:tok0 + stoks],
                                      start=True, stop=True),
                             waits=w, name="c_mm")
        for half in range(2):
            pG = pG0 if half == 0 else pG1
            pU = pU0 if half == 0 else pU1
            w = [dw_w[it]]
            if it >= 1:
                w.append(dv_h.get((it - 1, half)))
            elif s == 0:
                w.append(tr_tok[len(tiles) - 1])
                w.append(tr_tok[len(tiles) - 2])
            def gf(e, it=it, wb=wb, half=half, pG=pG):
                for k in range(KC):
                    ins = e.matmul(pG[:, 0:stoks],
                                   wbuf[:, wb + k * 512 + half * 128:wb + k * 512 + half * 128 + 128],
                                   xn_fm[:, k * W + tok0:k * W + tok0 + stoks],
                                   start=(k == 0), stop=(k == KC - 1))
                return ins
            def uf(e, it=it, wb=wb, half=half, pU=pU):
                for k in range(KC):
                    ins = e.matmul(pU[:, 0:stoks],
                                   wbuf[:, wb + k * 512 + 256 + half * 128:wb + k * 512 + 256 + half * 128 + 128],
                                   xn_fm[:, k * W + tok0:k * W + tok0 + stoks],
                                   start=(k == 0), stop=(k == KC - 1))
                return ins
            pe_g = S.add("tensor", gf, waits=w, name="g_mm")
            pe_gu[(it, half)] = S.add("tensor", uf, waits=[], name="u_mm")
            pe_gu_g[(it, half)] = pe_g

            w = [pe_gu_g[(it, half)], pe_gu[(it, half)]]
            if it >= 1:
                w.append(dv_h.get((it - 1, half)))
            ac_silu[(it, half)] = S.add("scalar", lambda e, half=half, pG=pG:
                                        e.activation(out=s_scr[:, half * SLAB:half * SLAB + stoks],
                                                     in_=pG[:, 0:stoks], func=AF.Silu),
                                        waits=w, name="silu")
            chunk = 2 * it + half
            hdst = h_buf[:, chunk * SLAB:chunk * SLAB + stoks]
            if routed:
                d1 = S.add("vector", lambda e, half=half, pU=pU:
                           e.tensor_mul(out=s_scr[:, half * SLAB:half * SLAB + stoks],
                                        in0=s_scr[:, half * SLAB:half * SLAB + stoks],
                                        in1=pU[:, 0:stoks]),
                           waits=[ac_silu[(it, half)]], name="hmul1")
                dv_h[(it, half)] = S.add("vector", lambda e, hdst=hdst, pC=pC, half=half:
                                         e.tensor_mul(out=hdst,
                                                      in0=s_scr[:, half * SLAB:half * SLAB + stoks],
                                                      in1=pC[:, 0:stoks]),
                                         waits=[d1, pe_c[it]], name="hmul2")
            else:
                dv_h[(it, half)] = S.add("vector", lambda e, half=half, pU=pU, hdst=hdst:
                                         e.tensor_mul(out=hdst,
                                                      in0=s_scr[:, half * SLAB:half * SLAB + stoks],
                                                      in1=pU[:, 0:stoks]),
                                         waits=[ac_silu[(it, half)]], name="hmul")
        if it + 2 < NIT:
            emit_wld(it + 2, [pe_gu[(it, 1)]])

    npass = (stc + 1) // 2
    lastX = None
    pD_free = None
    for p in range(npass):
        tcs = [tc for tc in (2 * p, 2 * p + 1) if tc < stc]
        dw_d = {}
        pe_d = {}

        def emit_dld(c, w):
            w = list(w) + [RD["down"][c] if c < 32 else RD["sd"][c - 32],
                           RD["down"][c + 1] if c + 1 < 32 else RD["sd"][c + 1 - 32]]
            tok = S.add("sync" if c % 4 == 0 else "scalar", lambda e, c=c:
                        [e.dma_start(out=dbuf[:, (c % 4) * D:(c % 4 + 2) * D]
                                     .rearrange("p (two c) -> p two c", c=D),
                                     in_=T["s_down"][c * 128:(c + 2) * 128, :]
                                     .rearrange("(two p) c -> p two c", p=128))],
                        waits=w, dma=1, sem="dbA" if c % 4 == 0 else "dbB",
                        name="d_ld")
            dw_d[c] = tok
            dw_d[c + 1] = tok
        if p >= 1:
            w0a, w0b = prev_pe
        elif db_free is not None:
            w0a, w0b = db_free
        else:
            w0a = w0b = pBtok
        emit_dld(0, [w0a])
        emit_dld(2, [w0b])
        for c in range(48):
            w = [dw_d[c], dv_h[(c // 2, c % 2)]]
            if c == 0:
                w.append(dv_h.get((NIT - 1, 1)))
                if p >= 1:
                    w.append(lastX)
                else:
                    w.append(pB_pD_free)
            def df(e, c=c, tcs=tcs):
                for ti, tc in enumerate(tcs):
                    pD = pD0 if ti == 0 else pD1
                    for nh in range(2):
                        ins = e.matmul(pD[:, nh * 512:(nh + 1) * 512],
                                       h_buf[:, c * SLAB + tc * 128:c * SLAB + tc * 128 + 128],
                                       dbuf[:, (c % 4) * D + nh * 512:(c % 4) * D + (nh + 1) * 512],
                                       start=(c == 0), stop=(c == 47))
                return ins
            pe_d[c] = S.add("tensor", df, waits=w, name="d_mm")
            if c % 2 == 1 and c + 3 < 48:
                emit_dld(c + 3, [pe_d[c]])
        xts = []
        for ti, tc in enumerate(tcs):
            pD = pD0 if ti == 0 else pD1
            t = tiles[(tok0 + tc * 128) // 128]
            xts.append(S.add("vector", lambda e, pD=pD, t=t:
                             e.scalar_tensor_tensor(out=X[:, t * D:(t + 1) * D],
                                                    in0=pD[:, 0:D], scalar=0.5,
                                                    in1=X[:, t * D:(t + 1) * D],
                                                    op0=ALU.mult, op1=ALU.add),
                             waits=[pe_d[47]], name="xupd"))
        lastX = xts[-1]
        pD_free = xts[-1]
        prev_pe = (pe_d[45], pe_d[47])
    return (lastX, pD_free, pe_gu[(NIT - 1, 1)], prev_pe)


pB_last_ped = {}


def _attention(nc, S, T, G, es, prev):
    pool_tot = [S.total("gpsA"), S.total("gpsB"), S.total("gpdA"), S.total("gpdB")]
    for _e in ("vector", "scalar", "sync", "gpsimd"):
        S.fence(_e, pool_tot, name="pool_gate")
    es2 = contextlib.ExitStack()
    with es2:
        sb2 = lambda name, shape, dt=F32: es2.enter_context(nc.sbuf_tensor("a" + name, shape, dt))
        K_fm = sb2("K", [128, KC * NTOK], BF16)
        V_tm = sb2("V", [128, NT * 1040], BF16)
        xn_fm = sb2("xn", [128, 2 * KC * 512], BF16)
        xn_tm = sb2("xt", [128, 4 * D], BF16)

        groups = [list(range(0, 4)), list(range(4, 8)), list(range(8, 12)),
                  list(range(12, 16)), list(range(16, 18))]

        qkv_done = []
        with contextlib.ExitStack() as es3:
            wv_sb = es3.enter_context(nc.sbuf_tensor("awv", [128, KC * D], BF16))
            wq_sb = es3.enter_context(nc.sbuf_tensor("awq", [128, 2 * 1024], BF16))
            qstage = es3.enter_context(nc.sbuf_tensor("aqs", [128, 2 * 512], BF16))
            pQK = es3.enter_context(nc.psum_tensor("apQK", [128, 512], F32))
            pQK2 = es3.enter_context(nc.psum_tensor("apQK2", [128, 512], F32))
            pV = es3.enter_context(nc.psum_tensor("apV", [128, D], F32))

            wv_tok = S.add("sync", lambda e: [
                e.dma_start(out=wv_sb[:, k * D:(k + 1) * D],
                            in_=T["s_wqkvV"][k * 128:(k + 1) * 128, :]) for k in range(KC)],
                waits=[prev, G["_ready"]["qkv"]], dma=KC, sem="wv", name="wv_ld")

            carry = {"done": {-2: prev, -1: prev}, "wq": None, "pqk_ev": None,
                     "sp": None, "vev": None}
            for gi, gtiles in enumerate(groups):
                xn_view = xn_fm[:, (gi % 2) * KC * 512:(gi % 2) * KC * 512 + KC * 512]
                asq = xn_tm[:, (gi % 2) * 2 * D:(gi % 2) * 2 * D + D]
                _qkv_group(nc, S, T, G, gi, gtiles, xn_view, xn_tm, asq, wq_sb,
                           wv_sb, qstage, K_fm, V_tm, pQK, pQK2, pV,
                           wv_tok, carry)
            qkv_done = carry["done"][len(groups) - 1]
            S.fence("scalar", [S.total("spA"), S.total("spB")], name="sp_fence")
        S.new_block()

        with contextlib.ExitStack() as es4:
            sb4 = lambda name, shape, dt=F32: es4.enter_context(nc.sbuf_tensor("b" + name, shape, dt))
            wo_sb = sb4("wo", [128, KC * D], BF16)
            qring = sb4("qr", [128, 2 * 1024], BF16)
            exps = sb4("ex", [128, 2 * 384], BF16)
            att_fm = sb4("ao", [128, 2 * 1024], BF16)
            t_av = sb4("tv", [128, 128])
            rrow = sb4("rw", [64, 128])
            rrowb = sb4("rb", [64, 128], BF16)
            pS = es4.enter_context(nc.psum_tensor("bpS", [128, 384], F32))
            pS2 = es4.enter_context(nc.psum_tensor("bpS2", [128, 384], F32))
            pAV = es4.enter_context(nc.psum_tensor("bpAV", [128, 128], F32))
            pNR = es4.enter_context(nc.psum_tensor("bpNR", [128, 256], F32))
            pO = es4.enter_context(nc.psum_tensor("bpO", [128, D], F32))

            wo_tok = S.add("sync", lambda e: [
                e.dma_start(out=wo_sb[:, k * D:(k + 1) * D],
                            in_=T["s_wout"][k * 128:(k + 1) * 128, :]) for k in range(KC)],
                waits=[qkv_done, G["_ready"]["wout"]], dma=KC, sem="wo", name="wo_ld")

            last = _qb_loop(nc, S, T, G, K_fm, V_tm, qring, exps, att_fm, t_av,
                            rrow, rrowb, wo_sb, pS, pS2, pAV, pNR, pO,
                            wo_tok, qkv_done)
        return last


def _qkv_group(nc, S, T, G, gi, gtiles, xn_fm, xn_tm, asq, wq_sb, wv_sb, qstage,
               K_fm, V_tm, pQK, pQK2, pV, wv_tok, carry):
    t0 = gtiles[0]
    gn = len(gtiles)
    gw = gn * 128

    prev = carry["done"][gi - 2]
    stats = _ln_stats(nc, S, G, gtiles, prev, asq, coff=t0)
    tr_tok = _ln_apply(nc, S, G, gtiles, stats, xn_fm, xn_tm, gw, prev,
                       nslots=2, coff=t0, sbase=(gi % 2) * 2,
                       prev_tr=carry.get("tr") or ())
    carry["tr"] = (tr_tok[max(0, gn - 2)], tr_tok[gn - 1])
    xn_ready = (tr_tok[gn - 1], tr_tok[max(0, gn - 2)])

    dw_q = {}
    pe_qk = {}
    ev_qk = {}
    sp_q = {}

    def emit_qkld(oc, w):
        dw_q[oc] = S.add("sync", lambda e, oc=oc:
                         [e.dma_start(out=wq_sb[:, (oc % 2) * 1024:(oc % 2 + 1) * 1024],
                                      in_=T["s_wqkvQK"][oc, :, :])],
                         waits=w, dma=1, sem="qkA" if oc % 2 == 0 else "qkB",
                         name="qk_ld")
    qkv_rd = G["_ready"]["qkv"]
    w0 = [qkv_rd]
    w1 = [qkv_rd]
    if carry["wq"]:
        w0.append(carry["wq"][0])
        w1.append(carry["wq"][1])
    else:
        w0.append(prev)
        w1.append(prev)
    emit_qkld(0, w0)
    emit_qkld(1, w1)
    for oc in range(16):
        p = pQK if oc % 2 == 0 else pQK2
        w = [dw_q[oc]] + list(xn_ready)
        if oc >= 2:
            w.append(ev_qk[oc - 2])
        elif carry["pqk_ev"]:
            w.append(carry["pqk_ev"][oc])
        def qf(e, oc=oc, p=p, gw=gw):
            for k in range(KC):
                ins = e.matmul(p[:, 0:gw],
                               wq_sb[:, (oc % 2) * 1024 + k * 128:(oc % 2) * 1024 + (k + 1) * 128],
                               xn_fm[:, k * gw:(k + 1) * gw],
                               start=(k == 0), stop=(k == KC - 1))
            return ins
        pe_qk[oc] = S.add("tensor", qf, waits=w, name="qk_mm")
        if oc + 2 < 16:
            emit_qkld(oc + 2, [pe_qk[oc]])
        if oc < 8:
            w2 = [pe_qk[oc]]
            if oc >= 2:
                w2.append(sp_q[oc - 2])
            elif carry["sp"]:
                w2.append(carry["sp"][oc])
            ev_qk[oc] = S.add("vector", lambda e, oc=oc, p=p, gw=gw:
                              e.tensor_scalar(out=qstage[:, (oc % 2) * 512:(oc % 2) * 512 + gw],
                                              in0=p[:, 0:gw],
                                              scalar1=G["bqk"][:, oc:oc + 1], scalar2=None,
                                              op0=ALU.add),
                              waits=w2, name="q_ev")
            own_lo = max(t0, 1)
            own_hi = min(gtiles[-1], 16)
            if own_hi >= own_lo:
                def spf(e, oc=oc, own_lo=own_lo, own_hi=own_hi, t0=t0):
                    return [e.dma_start(
                        out=T["q_scr"][oc, :, (own_lo - 1) * 128:own_hi * 128],
                        in_=qstage[:, (oc % 2) * 512 + (own_lo - t0) * 128:
                                   (oc % 2) * 512 + (own_hi - t0 + 1) * 128])]
                sp_q[oc] = S.add("scalar", spf, waits=[ev_qk[oc]], dma=1,
                                 sem="spA" if oc % 2 == 0 else "spB", name="q_sp")
            else:
                sp_q[oc] = ev_qk[oc]
        else:
            ev_qk[oc] = S.add("vector", lambda e, oc=oc, p=p, gw=gw, t0=t0:
                              e.tensor_scalar(out=K_fm[:, (oc - 8) * NTOK + t0 * 128:
                                                      (oc - 8) * NTOK + t0 * 128 + gw],
                                              in0=p[:, 0:gw],
                                              scalar1=G["bqk"][:, oc:oc + 1], scalar2=None,
                                              op0=ALU.add),
                              waits=[pe_qk[oc]], name="k_ev")

    pe_v = {}
    ev_v = {}
    for i, t in enumerate(gtiles):
        w = [wv_tok, tr_tok[i], pe_qk[15]]
        if i >= 1:
            w.append(ev_v[i - 1])
        elif carry["vev"]:
            w.append(carry["vev"])
        def vf(e, i=i, gw=gw):
            for k in range(KC):
                e.matmul(pV[:, 0:512],
                         xn_fm[:, k * gw + i * 128:k * gw + (i + 1) * 128],
                         wv_sb[:, k * D:k * D + 512], start=(k == 0), stop=False)
                e.matmul(pV[:, 512:1024],
                         xn_fm[:, k * gw + i * 128:k * gw + (i + 1) * 128],
                         wv_sb[:, k * D + 512:(k + 1) * D], start=(k == 0), stop=False)
            e.matmul(pV[:, 0:512], G["onesb"][0:1, :], G["bvb"][0:1, 0:512],
                     start=False, stop=True)
            ins = e.matmul(pV[:, 512:1024], G["onesb"][0:1, :], G["bvb"][0:1, 512:1024],
                           start=False, stop=True)
            return ins
        pe_v[i] = S.add("tensor", vf, waits=w, name="v_mm")
        m1 = S.add("vector", lambda e, t=t: e.memset(V_tm[:, t * 1040:(t + 1) * 1040], 1.0),
                   waits=[], name="v_ones")
        def vev(e, t=t):
            for h in range(NH):
                ins = e.tensor_copy(out=V_tm[:, t * 1040 + h * 65:t * 1040 + h * 65 + 64],
                                    in_=pV[:, h * 64:(h + 1) * 64])
            return ins
        ev_v[i] = S.add("vector", vev, waits=[pe_v[i], m1], name="v_ev")

    carry["wq"] = (pe_qk[14], pe_qk[15])
    carry["pqk_ev"] = (ev_qk[14], ev_qk[15])
    carry["sp"] = (sp_q[6], sp_q[7])
    carry["vev"] = ev_v[gn - 1]
    carry["done"][gi] = ev_v[gn - 1]


def _qb_loop(nc, S, T, G, K_fm, V_tm, qring, exps, att_fm, t_av, rrow, rrowb,
             wo_sb, pS, pS2, pAV, pNR, pO, wo_tok, qkv_done):
    dw_qr = {}
    sc_pe = {}
    exp_ac = {}
    msk_dv = {}
    av_pe = {}
    nm_pe = {}
    rc_dv = {}
    rb_ac = {}
    rb_pe = {}
    avev_ac = {}
    ao_dv = {}
    opj_pe = {}
    xu_dv = {}

    def emit_qld(j, w):
        dw_qr[j] = S.add("sync", lambda e, j=j:
                         [e.dma_start(out=qring[:, (j % 2) * 1024:(j % 2 + 1) * 1024]
                                      .rearrange("p (oc c) -> p oc c", c=128),
                                      in_=T["q_scr"].rearrange("oc p c -> p oc c")
                                      [:, :, (j - 1) * 128:j * 128])],
                         waits=w, dma=1, sem="qrA" if j % 2 == 0 else "qrB",
                         name="q_ld")
    spills = [S.total("spA"), S.total("spB")]
    emit_qld(1, [qkv_done] + spills)
    emit_qld(2, [qkv_done] + spills)

    for j in OWN:
        qb = (j % 2) * 1024
        for oc in range(8):
            for hi in range(2):
                pSc = pS if hi == 0 else pS2
                w = [dw_qr[j]]
                pk = (j, oc - 1, hi) if oc >= 1 else (j - 1, 7, hi)
                if exp_ac.get(pk):
                    w.append(exp_ac[pk])
                def scf(e, j=j, oc=oc, hi=hi, pSc=pSc, qb=qb):
                    for m, kb in enumerate((j - 1, j, j + 1)):
                        ins = e.matmul(
                            pSc[:, m * 128:(m + 1) * 128],
                            K_fm[hi * 64:(hi + 1) * 64,
                                 oc * NTOK + kb * 128:oc * NTOK + (kb + 1) * 128],
                            qring[hi * 64:(hi + 1) * 64, qb + oc * 128:qb + (oc + 1) * 128],
                            start=True, stop=True)
                    return ins
                sc_pe[(j, oc, hi)] = S.add("tensor", scf, waits=w, name="sc_mm")
                w = [sc_pe[(j, oc, hi)]]
                pk = (j, oc - 1, hi) if oc >= 1 else (j - 1, 7, hi)
                if av_pe.get(pk):
                    w.append(av_pe[pk])
                exp_ac[(j, oc, hi)] = S.add("scalar", lambda e, hi=hi, pSc=pSc:
                                            e.activation(out=exps[:, hi * 384:hi * 384 + 384],
                                                         in_=pSc[:, 0:384], func=AF.Exp,
                                                         scale=G["s32"]),
                                            waits=w, name="exp")
                mL = G["mLe"] if j == 1 else G["triL"]
                mR = G["mRe"] if j == 16 else G["triR"]
                d1 = S.add("vector", lambda e, hi=hi, mL=mL:
                           e.tensor_mul(out=exps[:, hi * 384:hi * 384 + 128],
                                        in0=exps[:, hi * 384:hi * 384 + 128], in1=mL[:, :]),
                           waits=[exp_ac[(j, oc, hi)]], name="mskL")
                msk_dv[(j, oc, hi)] = S.add("vector", lambda e, hi=hi, mR=mR:
                                            e.tensor_mul(out=exps[:, hi * 384 + 256:hi * 384 + 384],
                                                         in0=exps[:, hi * 384 + 256:hi * 384 + 384],
                                                         in1=mR[:, :]),
                                            waits=[d1], name="mskR")
            for hi in range(2):
                h = 2 * oc + hi
                w = [msk_dv[(j, oc, hi)]]
                pk = (j, oc - 1) if oc >= 1 else ((j - 1, 7) if j >= 2 else None)
                if pk and avev_ac.get(pk):
                    w.append(avev_ac[pk])
                def avf(e, j=j, h=h, hi=hi):
                    for m, kb in enumerate((j - 1, j, j + 1)):
                        ins = e.matmul(
                            pAV[hi * 64:(hi + 1) * 64, :],
                            V_tm[:, kb * 1040 + h * 65:kb * 1040 + h * 65 + 64],
                            exps[:, hi * 384 + m * 128:hi * 384 + (m + 1) * 128],
                            start=(m == 0), stop=(m == 2),
                            tile_position=(0, hi * 64))
                    return ins
                av_pe[(j, oc, hi)] = S.add("tensor", avf, waits=w, name="av_mm")
                w = [msk_dv[(j, oc, hi)]]
                pk = (j, oc - 1) if oc >= 1 else ((j - 1, 7) if j >= 2 else None)
                if pk and rc_dv.get(pk):
                    w.append(rc_dv[pk])
                if pk and ao_dv.get(pk):
                    w.append(ao_dv[pk])
                def nmf(e, j=j, h=h, hi=hi):
                    for m, kb in enumerate((j - 1, j, j + 1)):
                        ins = e.matmul(
                            pNR[hi * 32:hi * 32 + 1, 0:128],
                            V_tm[:, kb * 1040 + h * 65 + 64:kb * 1040 + h * 65 + 65],
                            exps[:, hi * 384 + m * 128:hi * 384 + (m + 1) * 128],
                            start=(m == 0), stop=(m == 2),
                            tile_position=(0, hi * 32))
                    return ins
                nm_pe[(j, oc, hi)] = S.add("tensor", nmf, waits=w, name="nm_mm")
            def rcf(e):
                e.reciprocal(out=rrow[0:1, 0:128], in_=pNR[0:1, 0:128])
                return e.reciprocal(out=rrow[32:33, 0:128], in_=pNR[32:33, 0:128])
            w = [nm_pe[(j, oc, 0)], nm_pe[(j, oc, 1)]]
            pk = (j, oc - 1) if oc >= 1 else ((j - 1, 7) if j >= 2 else None)
            if pk and rb_ac.get(pk):
                w.append(rb_ac[pk])
            rc_dv[(j, oc)] = S.add("vector", rcf, waits=w, name="recip")
            def rbf(e):
                e.copy(out=rrowb[0:1, 0:128], in_=rrow[0:1, 0:128])
                return e.copy(out=rrowb[32:33, 0:128], in_=rrow[32:33, 0:128])
            w = [rc_dv[(j, oc)]]
            if pk and rb_pe.get(pk):
                w.append(rb_pe[pk])
            rb_ac[(j, oc)] = S.add("scalar", rbf, waits=w, name="rcast")
            def rbm(e):
                e.matmul(pNR[0:64, 128:256], G["onesb"][0:1, 0:64], rrowb[0:1, 0:128],
                         start=True, stop=True)
                return e.matmul(pNR[64:128, 128:256], G["onesb"][32:33, 0:64],
                                rrowb[32:33, 0:128], start=True, stop=True,
                                tile_position=(32, 64))
            w = [rb_ac[(j, oc)]]
            pk2 = (j, oc - 1) if oc >= 1 else ((j - 1, 7) if j >= 2 else None)
            if pk2 and ao_dv.get(pk2):
                w.append(ao_dv[pk2])
            rb_pe[(j, oc)] = S.add("tensor", rbm, waits=w, name="rbc_mm")
            w = [av_pe[(j, oc, 0)], av_pe[(j, oc, 1)]]
            pk2 = (j, oc - 1) if oc >= 1 else ((j - 1, 7) if j >= 2 else None)
            if pk2 and ao_dv.get(pk2):
                w.append(ao_dv[pk2])
            avev_ac[(j, oc)] = S.add("scalar", lambda e: e.copy(out=t_av[:, :], in_=pAV[:, :]),
                                     waits=w, name="av_ev")
            w = [avev_ac[(j, oc)], rb_pe[(j, oc)]]
            if j >= 3:
                w.append(opj_pe.get(j - 2))
            ao_dv[(j, oc)] = S.add("vector", lambda e, j=j, oc=oc:
                                   e.tensor_mul(out=att_fm[:, (j % 2) * 1024 + oc * 128:
                                                          (j % 2) * 1024 + (oc + 1) * 128],
                                                in0=t_av[:, :], in1=pNR[:, 128:256]),
                                   waits=w, name="norm")
        if j + 2 <= 16:
            emit_qld(j + 2, [sc_pe[(j, 7, 1)]])
        w = [wo_tok, ao_dv[(j, 7)]]
        if j >= 2:
            w.append(xu_dv[j - 1])
        def opf(e, j=j):
            for k in range(KC):
                for nh in range(2):
                    e.matmul(pO[:, nh * 512:(nh + 1) * 512],
                             att_fm[:, (j % 2) * 1024 + k * 128:(j % 2) * 1024 + (k + 1) * 128],
                             wo_sb[:, k * D + nh * 512:k * D + (nh + 1) * 512],
                             start=(k == 0), stop=False)
            e.matmul(pO[:, 0:512], G["onesb"][0:1, :], G["bob"][0:1, 0:512],
                     start=False, stop=True)
            return e.matmul(pO[:, 512:1024], G["onesb"][0:1, :], G["bob"][0:1, 512:1024],
                            start=False, stop=True)
        opj_pe[j] = S.add("tensor", opf, waits=w, name="oproj")
        xu_dv[j] = S.add("vector", lambda e, j=j:
                         e.tensor_add(out=G["X"][:, j * D:(j + 1) * D],
                                      in0=pO[:, 0:D], in1=G["X"][:, j * D:(j + 1) * D]),
                         waits=[opj_pe[j]], name="xatt")
    return xu_dv[16]


def _host_constants():
    bf = ml_dtypes.bfloat16
    identb = np.eye(128, dtype=bf)
    identf = np.eye(128, dtype=np.float32)
    onehot = np.zeros((16, 2048), dtype=bf)
    for e in range(16):
        onehot[e, e * 128:(e + 1) * 128] = 1
    onesb = np.ones((128, 128), dtype=bf)
    i = np.arange(128)[:, None]
    j = np.arange(128)[None, :]
    triL = (i >= j).astype(bf)
    triR = (i <= j).astype(bf)
    return identb, identf, onehot, onesb, triL, triR


def _in_maps(inputs):
    x = np.ascontiguousarray(np.asarray(inputs["L0_attn_x"], dtype=np.float32))
    B_, Sq, C = x.shape
    xf = x.reshape(B_ * Sq, C)
    identb, identf, onehot, onesb, triL, triR = _host_constants()
    zeros128 = np.zeros((128, C), dtype=np.float32)
    zmask = np.zeros((128, 128), dtype=ml_dtypes.bfloat16)
    weights = {
        "Wqkv": np.ascontiguousarray(inputs["L0_attn_Wqkv"], dtype=np.float32),
        "bqkv": np.ascontiguousarray(inputs["L0_attn_bqkv"], dtype=np.float32),
        "Wout": np.ascontiguousarray(inputs["L0_attn_Wout"], dtype=np.float32),
        "bout": np.ascontiguousarray(inputs["L0_attn_bout"], dtype=np.float32),
        "router_w": np.ascontiguousarray(inputs["L0_ffn_router_w"], dtype=np.float32),
        "cbias": np.ascontiguousarray(inputs["L0_ffn_correction_bias"], dtype=np.float32),
        "cb128": np.ascontiguousarray(np.broadcast_to(
            np.asarray(inputs["L0_ffn_correction_bias"], dtype=np.float32)[None, :], (128, 16))),
        "gate_w": np.ascontiguousarray(inputs["L0_ffn_gate_w"], dtype=np.float32),
        "up_w": np.ascontiguousarray(inputs["L0_ffn_up_w"], dtype=np.float32),
        "down_w": np.ascontiguousarray(inputs["L0_ffn_down_w"], dtype=np.float32),
        "sgw": np.ascontiguousarray(inputs["L0_ffn_shared_gate_w"], dtype=np.float32),
        "suw": np.ascontiguousarray(inputs["L0_ffn_shared_up_w"], dtype=np.float32),
        "sdw": np.ascontiguousarray(inputs["L0_ffn_shared_down_w"], dtype=np.float32),
        "identb": identb, "identf": identf, "onehot": onehot, "onesb": onesb,
        "triL": triL, "triR": triR,
    }
    maps = []
    for c in range(8):
        seq = c // 2
        if c % 2 == 0:
            rows = np.concatenate([zeros128, xf[seq * 4096: seq * 4096 + 2176]], 0)
            mLe, mRe = zmask, triR
        else:
            rows = np.concatenate([xf[seq * 4096 + 1920: seq * 4096 + 4096], zeros128], 0)
            mLe, mRe = triL, zmask
        m = dict(weights)
        m["x_in"] = np.ascontiguousarray(rows)
        m["mLe"] = np.ascontiguousarray(mLe)
        m["mRe"] = np.ascontiguousarray(mRe)
        maps.append(m)
    return maps, (B_, Sq, C)


def kernel(**inputs):
    maps, (B_, Sq, C) = _in_maps(inputs)
    nc = build_nc("full")
    res = run_bass_kernel_spmd(nc, maps, core_ids=list(range(8)))
    out = np.concatenate([res.results[c]["y"] for c in range(8)], axis=0)
    return out.reshape(B_, Sq, C)
